# version 10
# baseline (speedup 1.0000x reference)
import numpy as np
import ml_dtypes
from contextlib import ExitStack

import concourse.bass as bass
import concourse.mybir as mybir
from concourse.bass_utils import run_bass_kernel_spmd
from concourse.bass import IndirectOffsetOnAxis as IOA

F32 = mybir.dt.float32
BF16 = mybir.dt.bfloat16
I32 = mybir.dt.int32
AX = mybir.AxisListType.X
AF = mybir.ActivationFunctionType
ALU = mybir.AluOpType

T = 2048
D = 1024
NH = 16
HD = 64
NPAIR = 8
NT = 16
NCT = 8
E = 8
FF = 4096
NQ = 8
NROUND = E * NQ
CAP = 512
NSI = 32
EPS = 1e-5
NCORE = 8


class _MI:
    def then_inc(self, *a, **k):
        return self

    def annotate(self, *a, **k):
        return self


class _ME:
    def __getattr__(self, name):
        return lambda *a, **k: _MI()


class _Wrap:

    def __init__(self, eng, ctr, passflag):
        object.__setattr__(self, "_e", eng)
        object.__setattr__(self, "_c", ctr)
        object.__setattr__(self, "_p", passflag)

    def wait_ge(self, *a, **k):
        return self._e.wait_ge(*a, **k)

    def __getattr__(self, name):
        f = getattr(self._e, name)
        c = self._c
        p = self._p
        e = self._e

        def g(*a, **k):
            if p[0] == 2 and c.n > 0:
                e.wait_ge(c.h, c.n)
            ins = f(*a, **k)
            c.inc(ins)
            return ins
        return g


class _Ctr:
    def __init__(self, stack, nc, name):
        self.h = stack.enter_context(nc.semaphore(name))
        self.n = 0

    def inc(self, inst, k=1):
        inst.then_inc(self.h, k)
        self.n += k
        return self.n


def _gammas():
    return 1.0 - 2.0 ** (-5.0 - np.arange(NH, dtype=np.float64))


def build(upto="full"):
    nc = bass.Bass("TRN2", target_bir_lowering=False)

    x_h = nc.dram_tensor("x", [T, D], F32, kind="ExternalInput")
    wqkv_h = nc.dram_tensor("wqkv", [D, 3 * D], F32, kind="ExternalInput")
    pkv_h = nc.dram_tensor("pkv", [128, NH * HD], F32, kind="ExternalInput")
    gnw_h = nc.dram_tensor("gnw", [128, D], F32, kind="ExternalInput")
    gnb_h = nc.dram_tensor("gnb", [128, D], F32, kind="ExternalInput")
    wr_h = nc.dram_tensor("wr", [128, NCT * E], F32, kind="ExternalInput")
    brb_h = nc.dram_tensor("brb", [128, E], F32, kind="ExternalInput")
    b1r_h = nc.dram_tensor("b1r", [128, E * 32], F32, kind="ExternalInput")
    b2aug_h = nc.dram_tensor("b2aug", [128, E * D], F32, kind="ExternalInput")
    w1_h = nc.dram_tensor("w1", [E, D, FF], F32, kind="ExternalInput")
    w2_h = nc.dram_tensor("w2", [E, FF, D], F32, kind="ExternalInput")
    out_h = nc.dram_tensor("out", [T, D], F32, kind="ExternalOutput")
    xs_d = nc.dram_tensor("xs_d", [8192, D], BF16, kind="Internal")
    ys_d = nc.dram_tensor("ys_d", [8192, D], F32, kind="Internal")

    gam = _gammas()
    tpow = np.arange(T, dtype=np.float64)
    gneg_np = np.zeros((NPAIR, 128, T), np.float64)
    for hp in range(NPAIR):
        gneg_np[hp, 0:64, :] = gam[2 * hp] ** (-tpow)
        gneg_np[hp, 64:128, :] = gam[2 * hp + 1] ** (-tpow)
    gneg_c = nc.inline_tensor(
        gneg_np.astype(np.float32).astype(ml_dtypes.bfloat16), name="gneg_c")
    gpow_np = np.zeros((128, NH * NT), np.float64)
    p_ar = np.arange(128, dtype=np.float64)
    for h in range(NH):
        for j in range(NT):
            gpow_np[:, h * NT + j] = gam[h] ** (128 * j + p_ar)
    gpow_c = nc.inline_tensor(gpow_np.astype(np.float32), name="gpow_c")
    ident_bf_c = nc.inline_tensor(np.eye(128, dtype=ml_dtypes.bfloat16), name="ib_c")
    ident_f_c = nc.inline_tensor(np.eye(128, dtype=np.float32), name="if_c")
    trimask_c = nc.inline_tensor(
        np.triu(np.ones((128, 128), np.float32)).astype(ml_dtypes.bfloat16), name="tm_c")
    striu_c = nc.inline_tensor(
        np.triu(np.ones((128, 128), np.float32), 1).astype(ml_dtypes.bfloat16),
        name="st_c")
    onesall_c = nc.inline_tensor(np.ones((128, 128), ml_dtypes.bfloat16), name="oa_c")
    ones128_c = nc.inline_tensor(np.ones((128, 128), np.float32), name="o1_c")
    zeros_c = nc.inline_tensor(np.zeros((1, 512), np.float32), name="z_c")
    epsc_c = nc.inline_tensor(np.full((128, 1), EPS, np.float32), name="eps_c")
    ecol_c = nc.inline_tensor(
        np.broadcast_to((np.arange(E) * CAP).astype(np.float32), (128, E)).copy(),
        name="ec_c")
    iotar_np = (np.arange(NSI)[None, :] * 128 + np.arange(128)[:, None]).astype(np.int32)
    iotar_c = nc.inline_tensor(iotar_np, name="ir_c")
    zbf_c = nc.inline_tensor(np.zeros((128, D), ml_dtypes.bfloat16), name="zb_c")

    with ExitStack() as S:
        ARENA = 211968
        arena_base = nc.bump_sbuf(ARENA)[0]
        _off = [0]

        def sba(name, shape, dtype):
            nbytes = int(np.prod(shape[1:])) * (2 if dtype == BF16 else 4)
            o = _off[0]
            _off[0] += (nbytes + 127) // 128 * 128
            assert _off[0] <= ARENA, (name, _off[0])
            return nc.alloc_sbuf_tensor_at(name, shape, dtype, offset=arena_base + o)

        racc = sba("racc", [128, NT * D], F32)
        xT = sba("xT", [128, NCT * T], BF16)
        x_st = sba("x_st", [128, 2 * D], F32)
        xbf_st = sba("xbf_st", [128, D], BF16)
        scr = sba("scr", [128, 2 * D], F32)
        gpow = sba("gpow", [128, NH * NT], F32)
        ident_bf = sba("ident_bf", [128, 128], BF16)
        ident_f = sba("ident_f", [128, 128], F32)
        trimask = sba("trimask", [128, 128], BF16)
        striu = sba("striu", [128, 128], BF16)
        onesall = sba("onesall", [128, 128], BF16)
        ones128 = sba("ones128", [128, 128], F32)
        zeros = sba("zeros", [1, 512], F32)
        gnw = sba("gnw", [128, D], F32)
        gnb = sba("gnb", [128, D], F32)
        wr = sba("wr", [128, NCT * E], F32)
        brb = sba("brb", [128, E], F32)
        b1r = sba("b1r", [128, E * 32], F32)
        pkv_bf = sba("pkv_bf", [128, NH * HD], BF16)
        sum_p = sba("sum_p", [128, NT], F32)
        sq_p = sba("sq_p", [128, NT], F32)
        stat2 = sba("stat2", [128, 2], F32)
        stat_m = sba("stat_m", [128, 2], F32)
        gvar = sba("gvar", [128, 1], F32)
        grstd = sba("grstd", [128, 1], F32)
        lncol = sba("lncol", [128, 8], F32)
        scores = sba("scores", [128, NT * E], F32)
        m16 = sba("m16", [128, NT], F32)
        dcol = sba("dcol", [128, E], F32)
        epsc = sba("epsc", [128, 1], F32)
        ecol = sba("ecol", [128, E], F32)
        iotar = sba("iotar", [128, NSI], I32)
        zbf = sba("zbf", [128, D], BF16)
        mprf = sba("mprf", [128, NT * E], F32)
        mprb = sba("mprb", [128, NT * E], BF16)
        csum = sba("csum", [128, E], F32)
        Rbuf = sba("Rbuf", [128, NT * E], F32)
        posf = sba("posf", [128, NT], F32)
        posi = sba("posi", [128, NT], I32)
        pcol = sba("pcol", [128, 2], F32)
        base_shared = _off[0]

        wq = [sba(f"wq_{b}", [128, 3 * NCT * 128], BF16) for b in range(2)]
        qT2 = [sba(f"qT2_{b}", [128, T], BF16) for b in range(2)]
        kT2 = [sba(f"kT2_{b}", [128, T], BF16) for b in range(2)]
        qhT2 = [sba(f"qhT2_{b}", [128, T], BF16) for b in range(2)]
        v2 = [sba(f"v2_{b}", [128, NT * 128], BF16) for b in range(2)]
        ret_sb = [sba(f"ret_{b}", [128, T], BF16) for b in range(2)]
        gneg = [sba(f"gneg_{b}", [128, T], BF16) for b in range(2)]
        attn_end = _off[0]

        _off[0] = base_shared
        w1q = [sba(f"w1q_{b}", [128, NCT * 512], BF16) for b in range(2)]
        w2q = [sba(f"w2q_{b}", [128, 4 * D], BF16) for b in range(2)]
        hbuf = [sba(f"hbuf_{b}", [128, 4 * 512], BF16) for b in range(2)]
        b2aug = sba("b2aug", [128, D], BF16)
        x1bft = [sba(f"x1bft_{b}", [128, D], BF16) for b in range(2)]
        xsT = [sba(f"xsT_{b}", [128, NCT * 512], BF16) for b in range(2)]
        ys_st = [sba(f"ys_st_{b}", [128, D], F32) for b in range(2)]
        ffn_end = _off[0]
        assert max(attn_end, ffn_end) <= ARENA, (attn_end, ffn_end)

        psF = S.enter_context(nc.psum_tensor("psF", [128, 4096], F32))
        ps_qkv = psF[:, 0:1024]
        ps_ret = psF[:, 1024:2048]
        ps_inner = psF[:, 2048:3072]
        ps_inner2 = psF[:, 3072:4096]
        ps_trbf = psF[:, 3072:4096].bitcast(BF16)
        ps_y = psF[:, 0:2048]
        ps_sc = psF[:, 3072:3200]
        ps_r = [psF[:, 3200:3208], psF[:, 3208:3216]]
        ps_yg = [psF[:, g * 512:(g + 1) * 512] for g in range(4)]
        ps_h2 = [psF[:, 2048:2560], psF[:, 2560:3072]]

        sconst = _Ctr(S, nc, "sconst")
        sx = [_Ctr(S, nc, f"sx{i}") for i in range(2)]
        sgn = [_Ctr(S, nc, f"sgn{i}") for i in range(2)]
        swq = [_Ctr(S, nc, f"swq{i}") for i in range(2)]
        swt = [_Ctr(S, nc, f"swt{i}") for i in range(2)]
        sb2 = _Ctr(S, nc, "sb2")
        sxz = _Ctr(S, nc, "sxz")
        sxt = [_Ctr(S, nc, f"sxt{i}") for i in range(2)]
        ssc = [_Ctr(S, nc, f"ssc{i}") for i in range(2)]
        sya = [_Ctr(S, nc, f"sya{i}") for i in range(4)]
        sga = _Ctr(S, nc, "sga")
        sst = _Ctr(S, nc, "sst")
        tk = _Ctr(S, nc, "tk")
        vk = _Ctr(S, nc, "vk")
        sk = _Ctr(S, nc, "sk")
        all_dma_ctrs = [sconst, *sx, *sgn, *swq, *swt, sb2, sxz, *sxt, *ssc,
                        *sya, sga, sst]
        PASS = [1]
        sync_plan = []
        gp_plan = []
        pidx = [0, 0]

        def plan_dma(plan, which, ctr, dst, src, kind="dma", **kw):
            if PASS[0] == 1:
                e = [dst, src, [], ctr, kind, kw]
                plan.append(e)
            else:
                e = plan[pidx[which]]
                pidx[which] += 1
            ctr.n += 16
            return (ctr, ctr.n), e

        def pwait(ent, ctr, val):
            if PASS[0] == 1 and val > 0:
                ent[2].append((ctr, val))

        def sync_dma(ctr, dst, src, **kw):
            return plan_dma(sync_plan, 0, ctr, dst, src, **kw)

        def gp_dma(ctr, dst, src, **kw):
            return plan_dma(gp_plan, 1, ctr, dst, src, **kw)

        last_wait = {}

        def wmk(engname, eng):
            def w(ctr, val):
                if val <= 0 or PASS[0] == 1:
                    return
                key = (engname, id(ctr))
                if last_wait.get(key, 0) >= val:
                    return
                eng.wait_ge(ctr.h, val)
                last_wait[key] = val
            return w

        tok = {}
        cast_tok = [0] * NT
        trmm_tok = [0] * NT
        trev_tok = [0] * NT
        pkv_cast = [0]
        qkv_mm_tok = [0] * NPAIR
        qkv_ev_tok = [0] * NPAIR
        attn_rd_tok = [0] * NPAIR
        qkv_ev_by_g = {}
        retmm_tok = {}
        ret_ev_by_rg = {}
        inner_wait_v = {}
        inner_wait_s = {}
        inner_mm = {}
        cross_tok = [0] * NH
        head_ev_tok = [0] * NH
        ln1_x2_tok = [0] * NT
        ln1_sq_tok = [0] * NT
        ln1_done_tok = [0] * NT
        stat2_tok = [0]
        stat_mm_tok = [0]
        gn_var_tok = [0]
        gn_std_tok = [0]
        gn_rstd_tok = [0]
        s_attn_done = [0]
        t_attn_done = [0]
        v_attn_done = [0]
        trx_tok = [0] * NT
        x1t_pre = [0] * NT
        sc_mm = [0] * NT
        mpr_tok = [0] * NT
        rmm_tok = [0] * NT
        posi_tok = [0]
        x1b_cast = [0] * NT
        hmm_tok = {}
        gelu_tok = {}
        w2g_tok = {}
        rnd_cons = [0] * NROUND
        qacc_tok = {}
        ln2_x2 = [0] * NT
        ln2_sq = [0] * NT
        ln2_done = [0] * NT

        const_pairs = [
            (ident_bf[:], ident_bf_c[:]), (ident_f[:], ident_f_c[:]),
            (trimask[:], trimask_c[:]), (striu[:], striu_c[:]),
            (onesall[:], onesall_c[:]),
            (ones128[:], ones128_c[:]), (zeros[:], zeros_c[:]),
            (gpow[:], gpow_c[:]), (gnw[:], gnw_h[:]), (gnb[:], gnb_h[:]),
            (wr[:], wr_h[:]), (brb[:], brb_h[:]), (b1r[:], b1r_h[:]),
            (epsc[:], epsc_c[:]), (ecol[:], ecol_c[:]), (iotar[:], iotar_c[:]),
            (zbf[:], zbf_c[:]),
            (scr[:, 0:NH * HD], pkv_h[:]),
        ]

        def make_plans():
            const_tok = None
            for d, s in const_pairs:
                const_tok, _ = sync_dma(sconst, d, s)
            xinit_tok, xinit_ent = [0] * NT, [None] * NT
            for tj in range(NT):
                xinit_tok[tj], xinit_ent[tj] = sync_dma(
                    sx[tj % 2],
                    x_st[:, (tj % 2) * D:(tj % 2 + 1) * D],
                    x_h[tj * 128:(tj + 1) * 128, :])
            gneg_tok, gneg_ent = [0] * NPAIR, [None] * NPAIR
            for hp in range(NPAIR):
                gneg_tok[hp], gneg_ent[hp] = sync_dma(sgn[hp % 2],
                                                      gneg[hp % 2][:], gneg_c[hp])
            for k in range(NSI):
                sync_dma(sxz, xs_d[k * 128:(k + 1) * 128, :], zbf[:])
            xrel_tok, xrel_ent = [0] * NT, [None] * NT
            for tj in range(NT):
                xrel_tok[tj], xrel_ent[tj] = sync_dma(
                    sx[tj % 2],
                    x_st[:, (tj % 2) * D:(tj % 2 + 1) * D],
                    x_h[tj * 128:(tj + 1) * 128, :])
            xsT_tok, xsT_ent = [0] * E, [None] * E
            for e_ in range(E if upto != "ln1" else 0):
                ents = []
                for ct in range(NCT):
                    xsT_tok[e_], ent = sync_dma(
                        sxt[e_ % 2],
                        xsT[e_ % 2][:, ct * 512:(ct + 1) * 512],
                        xs_d[e_ * CAP:(e_ + 1) * CAP, ct * 128:(ct + 1) * 128],
                        transpose=True)
                    ents.append(ent)
                xsT_ent[e_] = ents
            out_ent = [None] * NT
            for tj in range(NT):
                _, out_ent[tj] = sync_dma(sst, out_h[tj * 128:(tj + 1) * 128, :],
                                          racc[:, tj * D:(tj + 1) * D])
            wq_tok, wq_ent = [0] * NPAIR, [None] * NPAIR
            for hp in range(NPAIR):
                ents = []
                for g, cpt in enumerate((hp, 8 + hp, 16 + hp)):
                    src = wqkv_h[:, cpt * 128:(cpt + 1) * 128].rearrange(
                        "(ct p) u -> p ct u", p=128)
                    dst = wq[hp % 2].ap()[:, g * 1024:(g + 1) * 1024].rearrange(
                        "p (ct u) -> p ct u", u=128)
                    wq_tok[hp], e = gp_dma(swq[hp % 2], dst, src)
                    ents.append(e)
                wq_ent[hp] = ents
            scat_tok, scat_ent = [0] * NT, [None] * NT
            ya_tok, ya_ent = {}, {}
            gat_tok, gat_ent = [0] * NT, [None] * NT
            b2_tok, b2_ent = [0] * E, [None] * E
            rnd_tok, rnd_ent = [0] * NROUND, [None] * NROUND
            if upto != "ln1":
                for tj in range(NT):
                    scat_tok[tj], scat_ent[tj] = gp_dma(
                        ssc[tj % 2], xs_d[:], x1bft[tj % 2][:], kind="ind",
                        out_offset=IOA(posi[:, tj:tj + 1], 0))
                def emit_ya(r):
                    e_, q_ = r // NQ, r % NQ
                    for si in range(4):
                        u = r * 4 + si
                        ya_tok[(r, si)], ya_ent[(r, si)] = gp_dma(
                            sya[si], ys_d[:], ys_st[u % 2][:],
                            kind="ind",
                            out_offset=IOA(
                                iotar[:, e_ * 4 + si:e_ * 4 + si + 1], 0),
                            compute_op=(ALU.bypass if q_ == 0 else ALU.add))
                pend = {}
                for e_ in range(E):
                    b2_tok[e_], b2_ent[e_] = gp_dma(
                        sb2, b2aug[:], b2aug_h[:, e_ * D:(e_ + 1) * D])
                    for q_ in range(NQ):
                        r = e_ * NQ + q_
                        b = r % 2
                        src1 = w1_h[e_, :, q_ * 512:(q_ + 1) * 512].rearrange(
                            "(ct p) f -> p ct f", p=128)
                        dst1 = w1q[b].ap().rearrange("p (ct f) -> p ct f", f=512)
                        _, e1 = gp_dma(swt[b], dst1, src1)
                        src2 = w2_h[e_, q_ * 512:(q_ + 1) * 512, :].rearrange(
                            "(ft p) c -> p ft c", p=128)
                        dst2 = w2q[b].ap().rearrange("p (ft c) -> p ft c", c=1024)
                        rnd_tok[r], e2 = gp_dma(swt[b], dst2, src2)
                        rnd_ent[r] = (e1, e2)
                        pend[r + 1] = r
                        if r in pend:
                            emit_ya(pend.pop(r))
                for rel in sorted(pend):
                    emit_ya(pend[rel])
                for tj in range(NT):
                    gat_tok[tj], gat_ent[tj] = gp_dma(
                        sga, racc[:, tj * D:(tj + 1) * D],
                        ys_d[0:E * CAP, :],
                        kind="ind",
                        in_offset=IOA(posi[:, tj:tj + 1], 0),
                        compute_op=ALU.add)
            return dict(const_tok=const_tok, xinit_tok=xinit_tok, xinit_ent=xinit_ent,
                        gneg_tok=gneg_tok, gneg_ent=gneg_ent, xrel_tok=xrel_tok,
                        xrel_ent=xrel_ent, out_ent=out_ent, wq_tok=wq_tok,
                        wq_ent=wq_ent, rnd_tok=rnd_tok, rnd_ent=rnd_ent,
                        b2_tok=b2_tok, b2_ent=b2_ent, xsT_tok=xsT_tok,
                        xsT_ent=xsT_ent, scat_tok=scat_tok, scat_ent=scat_ent,
                        ya_tok=ya_tok, ya_ent=ya_ent, gat_tok=gat_tok,
                        gat_ent=gat_ent)

        def phA_scalar(scalar, P):
            w = wmk("s", scalar)
            w(*P["const_tok"])
            scalar.mul(pkv_bf[:], scr[:, 0:NH * HD], float(NH * HD))
            pkv_cast[0] = sk.n
            for tj in range(NT):
                w(*P["xinit_tok"][tj])
                if tj >= 1:
                    w(tk, trmm_tok[tj - 1])
                sl = (tj % 2) * D
                scalar.copy(xbf_st[:, 0:D], x_st[:, sl:sl + D])
                cast_tok[tj] = sk.n

        def phA_tensor(tensor, P):
            w = wmk("t", tensor)
            for tj in range(NT):
                w(sk, cast_tok[tj])
                if tj >= 2:
                    w(vk, trev_tok[tj - 2])
                pb = (tj % 2) * 1024
                for ct in range(NCT):
                    ins = tensor.transpose(
                        ps_trbf[:, pb + ct * 128:pb + (ct + 1) * 128],
                        xbf_st[:, ct * 128:(ct + 1) * 128], ident_bf[:])
                trmm_tok[tj] = tk.inc(ins)

        def phA_vector(vector, P):
            w = wmk("v", vector)
            for tj in range(NT):
                w(tk, trmm_tok[tj])
                pb = (tj % 2) * 1024
                src = ps_trbf[:, pb:pb + 1024].rearrange("p (ct u) -> p ct u", u=128)
                dst = xT.ap().rearrange("p (ct t) -> p ct t", t=T)[:, :, tj * 128:(tj + 1) * 128]
                vector.tensor_copy(dst, src)
                trev_tok[tj] = vk.n

        def _prev_key(h, i):
            if i >= 2:
                return (h, i - 2)
            if h >= 1:
                return (h - 1, NT - 2 + i)
            return None

        def phB_tensor(tensor, P):
            w = wmk("t", tensor)
            gctr = [0]
            rctr = [0]

            def qkv_pair(hp):
                bi = hp % 2
                w(*P["wq_tok"][hp])
                w(vk, trev_tok[NT - 1])
                if hp >= 2:
                    w(tk, attn_rd_tok[hp - 2])
                for g in range(2):
                    for c in range(4):
                        gidx = gctr[0]
                        w(vk, qkv_ev_by_g.get(gidx - 2, 0))
                        pbank = (gidx % 2) * 512
                        for ct in range(NCT):
                            ins = tensor.matmul(
                                ps_qkv[:, pbank:pbank + 512],
                                wq[bi][:, (g * 8 + ct) * 128:(g * 8 + ct + 1) * 128],
                                xT[:, ct * T + c * 512:ct * T + (c + 1) * 512],
                                start=(ct == 0), stop=(ct == NCT - 1))
                        retmm_tok[("qkv", hp, g, c)] = tk.inc(ins)
                        gctr[0] += 1
                for ti in range(NT):
                    gidx = gctr[0]
                    w(vk, qkv_ev_by_g.get(gidx - 2, 0))
                    pbank = (gidx % 2) * 512
                    for ct in range(NCT):
                        ins = tensor.matmul(
                            ps_qkv[:, pbank:pbank + 128],
                            xT[:, ct * T + ti * 128:ct * T + (ti + 1) * 128],
                            wq[bi][:, (2 * 8 + ct) * 128:(2 * 8 + ct + 1) * 128],
                            start=(ct == 0), stop=(ct == NCT - 1))
                    retmm_tok[("qkv", hp, 2, ti)] = tk.inc(ins)
                    gctr[0] += 1
                qkv_mm_tok[hp] = tk.n

            def attn_head(h):
                hp, po = h // 2, (h % 2) * 64
                bi = hp % 2
                psi = ps_inner if h % 2 == 0 else ps_inner2
                if h > 1:
                    w(vk, head_ev_tok[h - 2])
                tensor.matmul(psi[:, 0:512], zeros[0:1, 0:128],
                              zeros[0:1, 0:512], start=True, stop=True)
                tensor.matmul(psi[:, 512:1024], zeros[0:1, 0:128],
                              zeros[0:1, 0:512], start=True, stop=True)

                def ret_mms(i):
                    for c in range(i // 4, 4):
                        ridx = rctr[0]
                        wv, ws = ret_ev_by_rg.get(ridx - 2, (0, 0))
                        w(vk, wv)
                        w(sk, ws)
                        pbank = (ridx % 2) * 512
                        ins = tensor.matmul(
                            ps_ret[:, pbank:pbank + 512],
                            kT2[bi][po:po + 64, i * 128:(i + 1) * 128],
                            qT2[bi][po:po + 64, c * 512:(c + 1) * 512],
                            start=True, stop=True)
                        retmm_tok[(h, i, c)] = tk.inc(ins)
                        tok[("rbank", h, i, c)] = ridx
                        rctr[0] += 1

                def inner_mms(i):
                    w(vk, inner_wait_v.get((h, i), 0))
                    w(sk, inner_wait_s.get((h, i), 0))
                    for j in range(i, NT):
                        ins = tensor.matmul(
                            psi[:, j * 64:(j + 1) * 64],
                            ret_sb[i % 2][:, j * 128:(j + 1) * 128],
                            v2[bi][:, i * 128 + po:i * 128 + po + 64],
                            start=False, stop=False, skip_group_check=True)
                    inner_mm[(h, i)] = tk.inc(ins)

                ret_mms(0)
                for i in range(1, NT):
                    ret_mms(i)
                    inner_mms(i - 1)
                inner_mms(NT - 1)
                w(sk, pkv_cast[0])
                w(vk, qkv_ev_tok[hp])
                for j in range(NT):
                    ins = tensor.matmul(
                        psi[:, j * 64:(j + 1) * 64],
                        qhT2[bi][po:po + 64, j * 128:(j + 1) * 128],
                        pkv_bf[po:po + 64, h * 64:(h + 1) * 64],
                        start=False, stop=False, skip_group_check=True)
                cross_tok[h] = tk.inc(ins)

            qkv_pair(0)
            for hp in range(NPAIR):
                if hp + 1 < NPAIR:
                    qkv_pair(hp + 1)
                w(vk, qkv_ev_tok[hp])
                attn_head(2 * hp)
                attn_head(2 * hp + 1)
                attn_rd_tok[hp] = tk.n
            t_attn_done[0] = tk.n

        def phB_vector(vector, P):
            w = wmk("v", vector)
            gctr = [0]

            def qkv_evacs(hp):
                bi = hp % 2
                w(*P["gneg_tok"][hp])
                if hp >= 2:
                    w(tk, attn_rd_tok[hp - 2])
                for g in range(2):
                    for c in range(4):
                        w(tk, retmm_tok[("qkv", hp, g, c)])
                        pbank = (gctr[0] % 2) * 512
                        sl = slice(c * 512, (c + 1) * 512)
                        if g == 0:
                            vector.tensor_copy(qT2[bi][:, sl], ps_qkv[:, pbank:pbank + 512])
                            ins = vector.tensor_mul(qhT2[bi][:, sl],
                                                    ps_qkv[:, pbank:pbank + 512],
                                                    gneg[hp % 2][:, sl])
                        else:
                            ins = vector.tensor_mul(kT2[bi][:, sl],
                                                    ps_qkv[:, pbank:pbank + 512],
                                                    gneg[hp % 2][:, sl])
                        qkv_ev_by_g[gctr[0]] = vk.n
                        gctr[0] += 1
                for ti in range(NT):
                    w(tk, retmm_tok[("qkv", hp, 2, ti)])
                    pbank = (gctr[0] % 2) * 512
                    ins = vector.tensor_copy(v2[bi][:, ti * 128:(ti + 1) * 128],
                                             ps_qkv[:, pbank:pbank + 128])
                    qkv_ev_by_g[gctr[0]] = vk.n
                    gctr[0] += 1
                qkv_ev_tok[hp] = vk.n
                if PASS[0] == 1 and hp + 2 < NPAIR:
                    pwait(P["gneg_ent"][hp + 2], vk, vk.n)

            def attn_evacs(hp):
                for h in (2 * hp, 2 * hp + 1):
                    for i in range(NT):
                        c = i // 4
                        pk = _prev_key(h, i)
                        if pk is not None:
                            w(tk, inner_mm.get(pk, 0))
                        w(tk, retmm_tok[(h, i, c)])
                        ridx = tok[("rbank", h, i, c)]
                        pbank = (ridx % 2) * 512
                        off = (i - 4 * c) * 128
                        ins = vector.tensor_mul(
                            ret_sb[i % 2][:, i * 128:(i + 1) * 128],
                            ps_ret[:, pbank + off:pbank + off + 128], trimask[:])
                        if i + 1 < 4 * c + 4:
                            off2 = off + 128
                            ins = vector.tensor_copy(
                                ret_sb[i % 2][:, (i + 1) * 128:(4 * c + 4) * 128],
                                ps_ret[:, pbank + off2:pbank + 512])
                        t_ = vk.n
                        ret_ev_by_rg[ridx] = (t_, ret_ev_by_rg.get(ridx, (0, 0))[1])
                        inner_wait_v[(h, i)] = t_
                    w(tk, cross_tok[h])
                    psi = ps_inner if h % 2 == 0 else ps_inner2
                    for j in range(NT):
                        ins = vector.tensor_scalar_mul(
                            racc[:, j * D + h * 64:j * D + (h + 1) * 64],
                            psi[:, j * 64:(j + 1) * 64],
                            gpow[:, h * NT + j:h * NT + j + 1])
                    head_ev_tok[h] = vk.n

            qkv_evacs(0)
            for hp in range(NPAIR):
                if hp + 1 < NPAIR:
                    qkv_evacs(hp + 1)
                attn_evacs(hp)
            v_attn_done[0] = vk.n

        def phB_scalar(scalar, P):
            w = wmk("s", scalar)
            for hp in range(NPAIR):
                for h in (2 * hp, 2 * hp + 1):
                    for i in range(NT):
                        c0 = i // 4
                        for c in range(c0 + 1, 4):
                            pk = _prev_key(h, i)
                            if pk is not None:
                                w(tk, inner_mm.get(pk, 0))
                            w(tk, retmm_tok[(h, i, c)])
                            ridx = tok[("rbank", h, i, c)]
                            pbank = (ridx % 2) * 512
                            scalar.copy(
                                ret_sb[i % 2][:, 4 * c * 128:(4 * c + 4) * 128],
                                ps_ret[:, pbank:pbank + 512])
                            t_ = sk.n
                            ret_ev_by_rg[ridx] = (ret_ev_by_rg.get(ridx, (0, 0))[0], t_)
                            inner_wait_s[(h, i)] = t_
            s_attn_done[0] = sk.n

        def _x1t_evac(vector, w, tj):
            sl = (tj % 2) * D
            w(tk, t_attn_done[0])
            w(tk, trx_tok[tj])
            if tj >= 2:
                w(tk, sc_mm[tj - 2])
            pb = (tj % 2) * 1024
            vector.tensor_copy(scr[:, sl:sl + D], ps_y[:, pb:pb + 1024])
            x1t_pre[tj] = vk.n

        def phC_vector(vector, P):
            w = wmk("v", vector)
            w(vk, head_ev_tok[NH - 1])
            for tj in range(NT):
                ins = vector.tensor_reduce(sum_p[:, tj:tj + 1],
                                           racc[:, tj * D:(tj + 1) * D], AX, ALU.add)
            vector.tensor_reduce(stat2[:, 0:1], sum_p[:], AX, ALU.add)
            w(sk, tok.get("gn_sq", 0))
            ins = vector.tensor_reduce(stat2[:, 1:2], sq_p[:], AX, ALU.add)
            stat2_tok[0] = vk.n
            w(tk, stat_mm_tok[0])
            vector.tensor_scalar_mul(stat_m[:], ps_inner[:, 0:2], 1.0 / (T * D))
            vector.tensor_mul(gvar[:], stat_m[:, 0:1], stat_m[:, 0:1])
            vector.tensor_sub(gvar[:], stat_m[:, 1:2], gvar[:])
            gn_var_tok[0] = vk.n
            w(sk, gn_std_tok[0])
            vector.reciprocal(grstd[:], grstd[:])
            gn_rstd_tok[0] = vk.n
            for tj in range(NT):
                sl = (tj % 2) * D
                lb = (tj % 2) * 4
                vector.tensor_scalar(scr[:, sl:sl + D], racc[:, tj * D:(tj + 1) * D],
                                     stat_m[:, 0:1], grstd[:], ALU.subtract, ALU.mult)
                vector.tensor_mul(scr[:, sl:sl + D], scr[:, sl:sl + D], gnw[:])
                vector.tensor_add(scr[:, sl:sl + D], scr[:, sl:sl + D], gnb[:])
                w(*P["xrel_tok"][tj])
                vector.tensor_add(scr[:, sl:sl + D], scr[:, sl:sl + D],
                                  x_st[:, sl:sl + D])
                vector.tensor_reduce(lncol[:, lb:lb + 1], scr[:, sl:sl + D], AX, ALU.add)
                vector.tensor_scalar_mul(lncol[:, lb:lb + 1], lncol[:, lb:lb + 1], 1.0 / D)
                vector.tensor_scalar_sub(
                    scr[:, sl:sl + D], scr[:, sl:sl + D], lncol[:, lb:lb + 1])
                ln1_x2_tok[tj] = vk.n
                if PASS[0] == 1 and tj + 2 < NT:
                    pwait(P["xrel_ent"][tj + 2], vk, ln1_x2_tok[tj])
                w(sk, ln1_sq_tok[tj])
                vector.reciprocal(lncol[:, lb + 1:lb + 2], lncol[:, lb + 2:lb + 3])
                vector.tensor_scalar_mul(
                    racc[:, tj * D:(tj + 1) * D], scr[:, sl:sl + D],
                    lncol[:, lb + 1:lb + 2])
                ln1_done_tok[tj] = vk.n
                if upto != "ln1":
                    _x1t_evac(vector, w, tj)

        def phC_scalar(scalar, P):
            w = wmk("s", scalar)
            w(vk, head_ev_tok[NH - 1])
            tok["sqC"] = 0
            for tj in range(NT):
                w(sk, tok["sqC"])
                scalar.activation(xbf_st[:, 0:D], racc[:, tj * D:(tj + 1) * D],
                                  AF.Square, accum_out=sq_p[:, tj:tj + 1])
                tok["sqC"] = sk.n
            tok["gn_sq"] = sk.n
            w(vk, gn_var_tok[0])
            scalar.activation(grstd[:], gvar[:], AF.Sqrt, bias=epsc[:])
            gn_std_tok[0] = sk.n
            for tj in range(NT):
                sl = (tj % 2) * D
                lb = (tj % 2) * 4
                w(vk, ln1_x2_tok[tj])
                w(sk, tok["sqC"])
                scalar.activation(
                    xbf_st[:, 0:D], scr[:, sl:sl + D], AF.Square,
                    accum_out=lncol[:, lb + 2:lb + 3])
                tok["sqC"] = sk.n
                scalar.activation(
                    lncol[:, lb + 2:lb + 3], lncol[:, lb + 2:lb + 3], AF.Sqrt,
                    bias=epsc[:], scale=1.0 / D)
                ln1_sq_tok[tj] = sk.n

        def phC_tensor(tensor, P):
            w = wmk("t", tensor)
            w(vk, stat2_tok[0])
            ins = tensor.matmul(ps_inner[:, 0:2], ones128[:], stat2[:],
                                start=True, stop=True)
            stat_mm_tok[0] = tk.inc(ins)

        def phD_tensor(tensor, P):
            w = wmk("t", tensor)
            w(sk, s_attn_done[0])
            w(vk, v_attn_done[0])
            tensor.matmul(ps_sc[:], zeros[0:1, 0:128], zeros[0:1, 0:128],
                          start=True, stop=True)
            for tj in range(NT):
                w(vk, ln1_done_tok[tj])
                if tj >= 2:
                    w(vk, x1t_pre[tj - 2])
                pb = (tj % 2) * 1024
                for ct in range(NCT):
                    ins = tensor.transpose(
                        ps_y[:, pb + ct * 128:pb + (ct + 1) * 128],
                        racc[:, tj * D + ct * 128:tj * D + (ct + 1) * 128],
                        ident_f[:])
                trx_tok[tj] = tk.inc(ins)
                w(vk, x1t_pre[tj])
                sl = (tj % 2) * D
                for ct in range(NCT):
                    ins = tensor.matmul(ps_sc[:, tj * 8:(tj + 1) * 8],
                                        scr[:, sl + ct * 128:sl + (ct + 1) * 128],
                                        wr[:, ct * 8:(ct + 1) * 8],
                                        start=False, stop=(ct == NCT - 1),
                                        skip_group_check=True)
                sc_mm[tj] = tk.inc(ins)
            for tj in range(NT):
                w(vk, mpr_tok[tj])
                if tj >= 2:
                    w(vk, tok.get(("rev", tj - 2), 0))
                for jp in range(tj + 1):
                    ins = tensor.matmul(
                        ps_r[tj % 2][:],
                        striu[:] if jp == tj else onesall[:],
                        mprb[:, jp * E:(jp + 1) * E],
                        start=(jp == 0), stop=(jp == tj))
                rmm_tok[tj] = tk.inc(ins)

        def phD_vector(vector, P):
            w = wmk("v", vector)
            for tj in range(NT):
                w(tk, sc_mm[tj])
                vector.tensor_add(scores[:, tj * E:(tj + 1) * E],
                                  ps_sc[:, tj * E:(tj + 1) * E], brb[:])
                vector.tensor_reduce(m16[:, tj:tj + 1],
                                     scores[:, tj * E:(tj + 1) * E], AX, ALU.max)
                vector.tensor_scalar_sub(dcol[:], scores[:, tj * E:(tj + 1) * E],
                                         m16[:, tj:tj + 1])
                vector.tensor_single_scalar(
                    mprf[:, tj * E:(tj + 1) * E], dcol[:], 0.0, ALU.is_ge)
                vector.tensor_copy(csum[:, 0:1], mprf[:, tj * E:tj * E + 1])
                for e_ in range(1, E):
                    vector.tensor_add(csum[:, e_:e_ + 1], csum[:, e_ - 1:e_],
                                      mprf[:, tj * E + e_:tj * E + e_ + 1])
                vector.tensor_single_scalar(dcol[:], csum[:], 1.0, ALU.is_equal)
                vector.tensor_mul(mprf[:, tj * E:(tj + 1) * E],
                                  mprf[:, tj * E:(tj + 1) * E], dcol[:])
                vector.tensor_copy(mprb[:, tj * E:(tj + 1) * E],
                                   mprf[:, tj * E:(tj + 1) * E])
                mpr_tok[tj] = vk.n
            for tj in range(NT):
                w(tk, rmm_tok[tj])
                vector.tensor_copy(Rbuf[:, tj * E:(tj + 1) * E], ps_r[tj % 2][:])
                tok[("rev", tj)] = vk.n
                vector.tensor_mul(dcol[:], Rbuf[:, tj * E:(tj + 1) * E],
                                  mprf[:, tj * E:(tj + 1) * E])
                vector.tensor_reduce(pcol[:, 0:1], dcol[:], AX, ALU.add)
                vector.tensor_mul(dcol[:], mprf[:, tj * E:(tj + 1) * E], ecol[:])
                vector.tensor_reduce(pcol[:, 1:2], dcol[:], AX, ALU.add)
                vector.tensor_add(posf[:, tj:tj + 1], pcol[:, 0:1], pcol[:, 1:2])
            vector.tensor_copy(posi[:], posf[:])
            posi_tok[0] = vk.n

        def phD_scalar(scalar, P):
            w = wmk("s", scalar)
            for tj in range(NT):
                w(vk, ln1_done_tok[tj])
                if tj >= 2:
                    w(ssc[tj % 2], 16 * ((tj - 2) // 2 + 1))
                scalar.copy(x1bft[tj % 2][:], racc[:, tj * D:(tj + 1) * D])
                x1b_cast[tj] = sk.n

        def phE_tensor(tensor, P):
            w = wmk("t", tensor)
            for r in range(NROUND):
                e_, q_ = r // NQ, r % NQ
                b = r % 2
                w(*P["rnd_tok"][r])
                w(*P["xsT_tok"][e_])
                if q_ == NQ - 1:
                    w(*P["b2_tok"][e_])
                for ft in range(4):
                    w(sk, gelu_tok.get((r, ft - 2), 0))
                    if ft < 2:
                        w(sk, gelu_tok.get((r - 1, ft + 2), 0))
                    ph = ps_h2[ft % 2]
                    for ct in range(NCT):
                        ins = tensor.matmul(
                            ph[:],
                            w1q[b][:, ct * 512 + ft * 128:ct * 512 + (ft + 1) * 128],
                            xsT[e_ % 2][:, ct * 512:(ct + 1) * 512],
                            start=(ct == 0), stop=(ct == NCT - 1))
                    hmm_tok[(r, ft)] = tk.inc(ins)
                for si in range(4):
                    for chg in range(2):
                        gidx = r * 8 + si * 2 + chg
                        w(vk, qacc_tok.get(gidx - 4, 0))
                        pg = ps_yg[gidx % 4]
                        for ft in range(4):
                            w(sk, gelu_tok.get((r, ft), 0))
                            last = (ft == 3) and (q_ != NQ - 1)
                            ins = tensor.matmul(
                                pg[:],
                                hbuf[b][:, ft * 512 + si * 128:ft * 512 + si * 128 + 128],
                                w2q[b][:, ft * 1024 + chg * 512:ft * 1024 + chg * 512 + 512],
                                start=(ft == 0), stop=last)
                        if q_ == NQ - 1:
                            ins = tensor.matmul(
                                pg[:], onesall[:],
                                b2aug[:, chg * 512:chg * 512 + 512],
                                start=False, stop=True)
                        w2g_tok[(r, si, chg)] = tk.inc(ins)
                rnd_cons[r] = tk.n
                if PASS[0] == 1:
                    if r + 2 < NROUND:
                        pwait(P["rnd_ent"][r + 2][0], tk, rnd_cons[r])
                    if q_ == NQ - 1:
                        if e_ + 2 < E:
                            for ent in P["xsT_ent"][e_ + 2]:
                                pwait(ent, tk, rnd_cons[r])
                        if e_ + 1 < E:
                            pwait(P["b2_ent"][e_ + 1], tk, rnd_cons[r])

        def phE_scalar(scalar, P):
            w = wmk("s", scalar)
            for r in range(NROUND):
                e_, q_ = r // NQ, r % NQ
                b = r % 2
                for ft in range(4):
                    w(tk, hmm_tok[(r, ft)])
                    w(tk, rnd_cons[r - 2] if r >= 2 else 0)
                    scalar.activation(
                        hbuf[b][:, ft * 512:(ft + 1) * 512], ps_h2[ft % 2][:],
                        AF.Gelu,
                        bias=b1r[:, e_ * 32 + q_ * 4 + ft:e_ * 32 + q_ * 4 + ft + 1])
                    gelu_tok[(r, ft)] = sk.n

        def phE_vector(vector, P):
            w = wmk("v", vector)
            for r in range(NROUND):
                for si in range(4):
                    u = r * 4 + si
                    if u >= 2:
                        w(sya[(u - 2) % 4], 16 * ((u - 2) // 4 + 1))
                    for chg in range(2):
                        gidx = r * 8 + si * 2 + chg
                        w(tk, w2g_tok[(r, si, chg)])
                        pg = ps_yg[gidx % 4]
                        vector.tensor_copy(
                            ys_st[u % 2][:, chg * 512:chg * 512 + 512], pg[:])
                        qacc_tok[gidx] = vk.n
                    if PASS[0] == 1:
                        pwait(P["ya_ent"][(r, si)], vk, vk.n)

        def phF_vector(vector, P):
            w = wmk("v", vector)
            for tj in range(NT):
                sl = (tj % 2) * D
                lb = (tj % 2) * 4
                if upto != "ln1":
                    w(sga, 16 * NT)
                vector.tensor_reduce(lncol[:, lb:lb + 1],
                                     racc[:, tj * D:(tj + 1) * D], AX, ALU.add)
                vector.tensor_scalar_mul(lncol[:, lb:lb + 1], lncol[:, lb:lb + 1],
                                         1.0 / D)
                vector.tensor_scalar_sub(
                    scr[:, sl:sl + D], racc[:, tj * D:(tj + 1) * D],
                    lncol[:, lb:lb + 1])
                ln2_x2[tj] = vk.n
                w(sk, ln2_sq[tj])
                vector.reciprocal(lncol[:, lb + 1:lb + 2], lncol[:, lb + 2:lb + 3])
                vector.tensor_scalar_mul(
                    racc[:, tj * D:(tj + 1) * D], scr[:, sl:sl + D],
                    lncol[:, lb + 1:lb + 2])
                ln2_done[tj] = vk.n
                if PASS[0] == 1:
                    pwait(P["out_ent"][tj], vk, ln2_done[tj])

        def phF_scalar(scalar, P):
            w = wmk("s", scalar)
            tok["sqF"] = 0
            for tj in range(NT):
                sl = (tj % 2) * D
                lb = (tj % 2) * 4
                w(vk, ln2_x2[tj])
                w(sk, tok.get("sqF", 0))
                scalar.activation(
                    xbf_st[:, 0:D], scr[:, sl:sl + D], AF.Square,
                    accum_out=lncol[:, lb + 2:lb + 3])
                tok["sqF"] = sk.n
                scalar.activation(
                    lncol[:, lb + 2:lb + 3], lncol[:, lb + 2:lb + 3], AF.Sqrt,
                    bias=epsc[:], scale=1.0 / D)
                ln2_sq[tj] = sk.n

        def ln1_debug_stores(P):
            if PASS[0] == 1:
                for tj in range(NT):
                    pwait(P["out_ent"][tj], vk, ln1_done_tok[tj])

        P = None
        for pno in (1, 2):
            PASS[0] = pno
            for c in all_dma_ctrs + [tk, vk, sk]:
                c.n = 0
            pidx[0] = pidx[1] = 0
            last_wait.clear()
            Pnew = make_plans()
            if P is None:
                P = Pnew
            if pno == 1:
                m = _ME()
                mv = _Wrap(_ME(), vk, PASS)
                ms = _Wrap(_ME(), sk, PASS)
                phA_scalar(ms, P)
                phA_tensor(m, P)
                phA_vector(mv, P)
                for tj in range(2, NT):
                    pwait(P["xinit_ent"][tj], sk, cast_tok[tj - 2])
                phB_tensor(m, P)
                phB_vector(mv, P)
                phB_scalar(ms, P)
                for hp in range(2, NPAIR):
                    for e in P["wq_ent"][hp]:
                        pwait(e, tk, qkv_mm_tok[hp - 2])
                if upto != "ln1":
                    for r in (0, 1):
                        pwait(P["rnd_ent"][r][0], tk, t_attn_done[0])
                        pwait(P["rnd_ent"][r][0], vk, v_attn_done[0])
                        pwait(P["rnd_ent"][r][0], sk, s_attn_done[0])
                    for e_ in (0, 1):
                        for ent in P["xsT_ent"][e_]:
                            pwait(ent, tk, t_attn_done[0])
                            pwait(ent, vk, v_attn_done[0])
                            pwait(ent, sk, s_attn_done[0])
                    pwait(P["b2_ent"][0], tk, t_attn_done[0])
                    pwait(P["b2_ent"][0], vk, v_attn_done[0])
                    pwait(P["b2_ent"][0], sk, s_attn_done[0])
                for tj in range(NT):
                    pwait(P["xrel_ent"][tj], sk, cast_tok[14 + (tj % 2)])
                phC_vector(mv, P)
                phC_scalar(ms, P)
                phC_tensor(m, P)
                if upto == "ln1":
                    ln1_debug_stores(P)
                else:
                    phD_tensor(m, P)
                    phD_vector(mv, P)
                    phD_scalar(ms, P)
                    for tj in range(NT):
                        ent = P["scat_ent"][tj]
                        pwait(ent, sxz, 16 * NSI)
                        pwait(ent, vk, posi_tok[0])
                        pwait(ent, sk, x1b_cast[tj])
                    for e_ in range(E):
                        for ent in P["xsT_ent"][e_]:
                            pwait(ent, ssc[0], 16 * (NT // 2))
                            pwait(ent, ssc[1], 16 * (NT // 2))
                    phE_tensor(m, P)
                    phE_scalar(ms, P)
                    phE_vector(mv, P)
                    for r in range(NROUND):
                        for si in range(4):
                            u = r * 4 + si
                            if u >= 4:
                                pwait(P["ya_ent"][(r, si)], sya[si],
                                      16 * (u // 4))
                    for tj in range(NT):
                        for si in range(4):
                            pwait(P["gat_ent"][tj], sya[si], 16 * NROUND)
                    phF_vector(mv, P)
                    phF_scalar(ms, P)
            else:
                block = S.enter_context(nc.Block())

                def t_body(tensor):
                    phA_tensor(tensor, P)
                    phB_tensor(tensor, P)
                    phC_tensor(tensor, P)
                    if upto != "ln1":
                        phD_tensor(tensor, P)
                        phE_tensor(tensor, P)

                def v_body(vector):
                    vector = _Wrap(vector, vk, PASS)
                    phA_vector(vector, P)
                    phB_vector(vector, P)
                    phC_vector(vector, P)
                    if upto != "ln1":
                        phD_vector(vector, P)
                        phE_vector(vector, P)
                        phF_vector(vector, P)
                    elif upto == "ln1":
                        pass

                def s_body(scalar):
                    scalar = _Wrap(scalar, sk, PASS)
                    phA_scalar(scalar, P)
                    phB_scalar(scalar, P)
                    phC_scalar(scalar, P)
                    if upto != "ln1":
                        phD_scalar(scalar, P)
                        phE_scalar(scalar, P)
                        phF_scalar(scalar, P)

                def sync_body(sync):
                    for dst, src, waits, ctr, kind, kw in sync_plan:
                        for (c, v) in waits:
                            sync.wait_ge(c.h, v)
                        sync.dma_start(dst, src, **kw).then_inc(ctr.h, 16)
                    sync.wait_ge(sst.h, sst.n)

                def gp_body(gpsimd):
                    for dst, src, waits, ctr, kind, kw in gp_plan:
                        for (c, v) in waits:
                            gpsimd.wait_ge(c.h, v)
                        if kind == "dma":
                            gpsimd.dma_start(dst, src, **kw).then_inc(ctr.h, 16)
                        else:
                            gpsimd.indirect_dma_start(
                                out=dst, out_offset=kw.get("out_offset"),
                                in_=src, in_offset=kw.get("in_offset"),
                                compute_op=kw.get("compute_op", ALU.bypass),
                            ).then_inc(ctr.h, 16)
                    if upto != "ln1":
                        gpsimd.wait_ge(sga.h, sga.n)

                block.tensor(t_body)
                block.vector(v_body)
                block.scalar(s_body)
                block.sync(sync_body)
                block.gpsimd(gp_body)

    return nc


def _host_layout(inputs):
    x = np.asarray(inputs["L0_attn_x"], dtype=np.float32)
    pkv = np.asarray(inputs["L0_attn_past_kv"], dtype=np.float32)
    wqkv = np.ascontiguousarray(np.asarray(inputs["L0_attn_W_qkv"], dtype=np.float32))
    gnw = np.asarray(inputs["L0_attn_gn_weight"], dtype=np.float32)
    gnb = np.asarray(inputs["L0_attn_gn_bias"], dtype=np.float32)
    wrv = np.asarray(inputs["L0_ffn_wr"], dtype=np.float32)
    br = np.asarray(inputs["L0_ffn_br"], dtype=np.float32)
    w1 = np.ascontiguousarray(np.asarray(inputs["L0_ffn_w1"], dtype=np.float32))
    b1 = np.asarray(inputs["L0_ffn_b1"], dtype=np.float32)
    w2 = np.ascontiguousarray(np.asarray(inputs["L0_ffn_w2"], dtype=np.float32))
    b2 = np.asarray(inputs["L0_ffn_b2"], dtype=np.float32)

    pkv_r1 = pkv.transpose(1, 0, 2).reshape(HD, NH * HD)
    pkv_r = np.ascontiguousarray(np.vstack([pkv_r1, pkv_r1]))
    gnw_b = np.ascontiguousarray(
        np.broadcast_to(np.repeat(gnw, HD)[None, :], (128, D)))
    gnb_b = np.ascontiguousarray(
        np.broadcast_to(np.repeat(gnb, HD)[None, :], (128, D)))
    wr_r = np.ascontiguousarray(
        wrv.reshape(NCT, 128, E).transpose(1, 0, 2).reshape(128, NCT * E))
    brb = np.ascontiguousarray(np.broadcast_to(br[None, :], (128, E)))
    b1r = np.ascontiguousarray(
        b1.reshape(E, 32, 128).transpose(2, 0, 1).reshape(128, E * 32))
    b2aug = np.zeros((128, E * D), np.float32)
    b2aug[0, :] = b2.reshape(-1)

    shared = dict(wqkv=wqkv, pkv=pkv_r, gnw=gnw_b, gnb=gnb_b, wr=wr_r, brb=brb,
                  b1r=b1r, b2aug=b2aug, w1=w1, w2=w2)
    maps = []
    for i in range(NCORE):
        m = dict(shared)
        m["x"] = np.ascontiguousarray(x[i])
        maps.append(m)
    return maps


def kernel(**inputs):
    maps = _host_layout(inputs)
    nc = build("full")
    res = run_bass_kernel_spmd(nc, maps, core_ids=list(range(NCORE)))
    return np.stack([np.asarray(res.results[i]["out"], dtype=np.float32)
                     for i in range(NCORE)], axis=0)


# version 12
# speedup vs baseline: 1.1731x; 1.1731x over previous
import numpy as np
import ml_dtypes
from contextlib import ExitStack

import concourse.bass as bass
import concourse.mybir as mybir
from concourse.bass_utils import run_bass_kernel_spmd
from concourse.bass import IndirectOffsetOnAxis as IOA

F32 = mybir.dt.float32
BF16 = mybir.dt.bfloat16
I32 = mybir.dt.int32
AX = mybir.AxisListType.X
AF = mybir.ActivationFunctionType
ALU = mybir.AluOpType

T = 2048
D = 1024
NH = 16
HD = 64
NPAIR = 8
NT = 16
NCT = 8
E = 8
FF = 4096
NQ = 8
NROUND = E * NQ
CAP = 512
NSI = 32
EPS = 1e-5
NCORE = 8


class _MI:
    def then_inc(self, *a, **k):
        return self

    def annotate(self, *a, **k):
        return self


class _ME:
    def __getattr__(self, name):
        return lambda *a, **k: _MI()


class _Wrap:

    def __init__(self, eng, ctr, passflag):
        object.__setattr__(self, "_e", eng)
        object.__setattr__(self, "_c", ctr)
        object.__setattr__(self, "_p", passflag)

    def wait_ge(self, *a, **k):
        return self._e.wait_ge(*a, **k)

    def __getattr__(self, name):
        f = getattr(self._e, name)
        c = self._c
        p = self._p
        e = self._e

        def g(*a, **k):
            if p[0] == 2 and c.n > 0:
                e.wait_ge(c.h, c.n)
            ins = f(*a, **k)
            c.inc(ins)
            return ins
        return g


class _Ctr:
    def __init__(self, stack, nc, name):
        self.h = stack.enter_context(nc.semaphore(name))
        self.n = 0

    def inc(self, inst, k=1):
        inst.then_inc(self.h, k)
        self.n += k
        return self.n


def _gammas():
    return 1.0 - 2.0 ** (-5.0 - np.arange(NH, dtype=np.float64))


def build(upto="full"):
    nc = bass.Bass("TRN2", target_bir_lowering=False)

    x_h = nc.dram_tensor("x", [T, D], F32, kind="ExternalInput")
    wqkv_h = nc.dram_tensor("wqkv", [D, 3 * D], F32, kind="ExternalInput")
    pkv_h = nc.dram_tensor("pkv", [128, NH * HD], F32, kind="ExternalInput")
    gnw_h = nc.dram_tensor("gnw", [128, NH], F32, kind="ExternalInput")
    gnb_h = nc.dram_tensor("gnb", [128, NH], F32, kind="ExternalInput")
    wr_h = nc.dram_tensor("wr", [128, NCT * E], F32, kind="ExternalInput")
    brb_h = nc.dram_tensor("brb", [128, E], F32, kind="ExternalInput")
    b1r_h = nc.dram_tensor("b1r", [128, E * 32], F32, kind="ExternalInput")
    b2aug_h = nc.dram_tensor("b2aug", [128, E * D], F32, kind="ExternalInput")
    w1_h = nc.dram_tensor("w1", [E, D, FF], F32, kind="ExternalInput")
    w2_h = nc.dram_tensor("w2", [E, FF, D], F32, kind="ExternalInput")
    out_h = nc.dram_tensor("out", [T, D], F32, kind="ExternalOutput")
    xs_d = nc.dram_tensor("xs_d", [8192, D], BF16, kind="Internal")
    ys_d = nc.dram_tensor("ys_d", [8192, D], F32, kind="Internal")

    gam = _gammas()
    tpow = np.arange(T, dtype=np.float64)
    gneg_np = np.zeros((NPAIR, 128, T), np.float64)
    for hp in range(NPAIR):
        gneg_np[hp, 0:64, :] = gam[2 * hp] ** (-tpow)
        gneg_np[hp, 64:128, :] = gam[2 * hp + 1] ** (-tpow)
    gneg_c = nc.inline_tensor(
        gneg_np.astype(np.float32).astype(ml_dtypes.bfloat16), name="gneg_c")
    gpow_np = np.zeros((128, NH * NT), np.float64)
    p_ar = np.arange(128, dtype=np.float64)
    for h in range(NH):
        for j in range(NT):
            gpow_np[:, h * NT + j] = gam[h] ** (128 * j + p_ar)
    gpow_c = nc.inline_tensor(gpow_np.astype(np.float32), name="gpow_c")
    ident_bf_c = nc.inline_tensor(np.eye(128, dtype=ml_dtypes.bfloat16), name="ib_c")
    ident_f_c = nc.inline_tensor(np.eye(128, dtype=np.float32), name="if_c")
    trimask_c = nc.inline_tensor(
        np.triu(np.ones((128, 128), np.float32)).astype(ml_dtypes.bfloat16), name="tm_c")
    striu_c = nc.inline_tensor(
        np.triu(np.ones((128, 128), np.float32), 1).astype(ml_dtypes.bfloat16),
        name="st_c")
    onesall_c = nc.inline_tensor(np.ones((128, 128), ml_dtypes.bfloat16), name="oa_c")
    ones128_c = nc.inline_tensor(np.ones((128, 128), np.float32), name="o1_c")
    zeros_c = nc.inline_tensor(np.zeros((1, 512), np.float32), name="z_c")
    epsc_c = nc.inline_tensor(np.full((128, 1), EPS, np.float32), name="eps_c")
    ecol_c = nc.inline_tensor(
        np.broadcast_to((np.arange(E) * CAP).astype(np.float32), (128, E)).copy(),
        name="ec_c")
    iotar_np = (np.arange(NSI)[None, :] * 128 + np.arange(128)[:, None]).astype(np.int32)
    iotar_c = nc.inline_tensor(iotar_np, name="ir_c")
    zbf_c = nc.inline_tensor(np.zeros((128, D), ml_dtypes.bfloat16), name="zb_c")

    with ExitStack() as S:
        ARENA = 211968
        arena_base = nc.bump_sbuf(ARENA)[0]
        _off = [0]

        def sba(name, shape, dtype):
            nbytes = int(np.prod(shape[1:])) * (2 if dtype == BF16 else 4)
            o = _off[0]
            _off[0] += (nbytes + 127) // 128 * 128
            assert _off[0] <= ARENA, (name, _off[0])
            return nc.alloc_sbuf_tensor_at(name, shape, dtype, offset=arena_base + o)

        racc = sba("racc", [128, NT * D], F32)
        xT = sba("xT", [128, NCT * T], BF16)
        x_st = sba("x_st", [128, 2 * D], F32)
        xbf_st = sba("xbf_st", [128, D], BF16)
        scr = sba("scr", [128, 2 * D], F32)
        gpow = sba("gpow", [128, NH * NT], F32)
        ident_bf = sba("ident_bf", [128, 128], BF16)
        ident_f = sba("ident_f", [128, 128], F32)
        trimask = sba("trimask", [128, 128], BF16)
        striu = sba("striu", [128, 128], BF16)
        onesall = sba("onesall", [128, 128], BF16)
        ones128 = sba("ones128", [128, 128], F32)
        zeros = sba("zeros", [1, 512], F32)
        gnw = sba("gnw", [128, NH], F32)
        gnb = sba("gnb", [128, NH], F32)
        wr = sba("wr", [128, NCT * E], F32)
        brb = sba("brb", [128, E], F32)
        b1r = sba("b1r", [128, E * 32], F32)
        pkv_bf = sba("pkv_bf", [128, NH * HD], BF16)
        sum_p = sba("sum_p", [128, NT], F32)
        sq_p = sba("sq_p", [128, NT], F32)
        stat2 = sba("stat2", [128, 2], F32)
        stat_m = sba("stat_m", [128, 2], F32)
        gvar = sba("gvar", [128, 1], F32)
        grstd = sba("grstd", [128, 1], F32)
        lncol = sba("lncol", [128, 8], F32)
        scores = sba("scores", [128, NT * E], F32)
        m16 = sba("m16", [128, NT], F32)
        dcol = sba("dcol", [128, E], F32)
        epsc = sba("epsc", [128, 1], F32)
        ecol = sba("ecol", [128, E], F32)
        iotar = sba("iotar", [128, NSI], I32)
        zbf = sba("zbf", [128, D], BF16)
        mprf = sba("mprf", [128, NT * E], F32)
        mprb = sba("mprb", [128, NT * E], BF16)
        csum = sba("csum", [128, E], F32)
        Rbuf = sba("Rbuf", [128, NT * E], F32)
        posf = sba("posf", [128, NT], F32)
        posi = sba("posi", [128, NT], I32)
        pcol = sba("pcol", [128, 2], F32)
        base_shared = _off[0]

        wq = [sba(f"wq_{b}", [128, 3 * NCT * 128], BF16) for b in range(2)]
        qT2 = [sba(f"qT2_{b}", [128, T], BF16) for b in range(2)]
        kT2 = [sba(f"kT2_{b}", [128, T], BF16) for b in range(2)]
        qhT2 = [sba(f"qhT2_{b}", [128, T], BF16) for b in range(2)]
        v2 = [sba(f"v2_{b}", [128, NT * 128], BF16) for b in range(2)]
        ret_sb = [sba(f"ret_{b}", [128, T], BF16) for b in range(2)]
        gneg = [sba(f"gneg_{b}", [128, T], BF16) for b in range(2)]
        attn_end = _off[0]

        _off[0] = base_shared
        w1q = [sba(f"w1q_{b}", [128, NCT * 512], BF16) for b in range(2)]
        w2q = [sba(f"w2q_{b}", [128, 4 * D], BF16) for b in range(2)]
        hbuf = [sba(f"hbuf_{b}", [128, 4 * 512], BF16) for b in range(2)]
        b2aug = sba("b2aug", [128, D], BF16)
        x1bft = [sba(f"x1bft_{b}", [128, D], BF16) for b in range(2)]
        xsT = [sba(f"xsT_{b}", [128, NCT * 512], BF16) for b in range(2)]
        ys_qacc = sba("ys_qacc", [128, 4 * D], F32)
        ffn_end = _off[0]
        assert max(attn_end, ffn_end) <= ARENA, (attn_end, ffn_end)

        psF = S.enter_context(nc.psum_tensor("psF", [128, 4096], F32))
        ps_qkv = psF[:, 0:1024]
        ps_ret = psF[:, 1024:2048]
        ps_inner = psF[:, 2048:3072]
        ps_inner2 = psF[:, 3072:4096]
        ps_trbf = psF[:, 3072:4096].bitcast(BF16)
        ps_y = psF[:, 0:2048]
        ps_sc = psF[:, 3072:3200]
        ps_r = [psF[:, 3200:3208], psF[:, 3208:3216]]
        ps_yg = [psF[:, g * 512:(g + 1) * 512] for g in range(4)]
        ps_h2 = [psF[:, 2048:2560], psF[:, 2560:3072]]

        sconst = _Ctr(S, nc, "sconst")
        sx = [_Ctr(S, nc, f"sx{i}") for i in range(2)]
        sgn = [_Ctr(S, nc, f"sgn{i}") for i in range(2)]
        swq = [_Ctr(S, nc, f"swq{i}") for i in range(2)]
        swt = [_Ctr(S, nc, f"swt{i}") for i in range(2)]
        sb2 = _Ctr(S, nc, "sb2")
        sxz = _Ctr(S, nc, "sxz")
        sxt = [_Ctr(S, nc, f"sxt{i}") for i in range(2)]
        ssc = [_Ctr(S, nc, f"ssc{i}") for i in range(2)]
        sya = [_Ctr(S, nc, f"sya{i}") for i in range(4)]
        sga = _Ctr(S, nc, "sga")
        sst = _Ctr(S, nc, "sst")
        tk = _Ctr(S, nc, "tk")
        vk = _Ctr(S, nc, "vk")
        sk = _Ctr(S, nc, "sk")
        all_dma_ctrs = [sconst, *sx, *sgn, *swq, *swt, sb2, sxz, *sxt, *ssc,
                        *sya, sga, sst]
        PASS = [1]
        sync_plan = []
        gp_plan = []
        pidx = [0, 0]

        def plan_dma(plan, which, ctr, dst, src, kind="dma", **kw):
            if PASS[0] == 1:
                e = [dst, src, [], ctr, kind, kw]
                plan.append(e)
            else:
                e = plan[pidx[which]]
                pidx[which] += 1
            ctr.n += 16
            return (ctr, ctr.n), e

        def pwait(ent, ctr, val):
            if PASS[0] == 1 and val > 0:
                ent[2].append((ctr, val))

        def sync_dma(ctr, dst, src, **kw):
            return plan_dma(sync_plan, 0, ctr, dst, src, **kw)

        def gp_dma(ctr, dst, src, **kw):
            return plan_dma(gp_plan, 1, ctr, dst, src, **kw)

        last_wait = {}

        def wmk(engname, eng):
            def w(ctr, val):
                if val <= 0 or PASS[0] == 1:
                    return
                key = (engname, id(ctr))
                if last_wait.get(key, 0) >= val:
                    return
                eng.wait_ge(ctr.h, val)
                last_wait[key] = val
            return w

        tok = {}
        cast_tok = [0] * NT
        trmm_tok = [0] * NT
        trev_tok = [0] * NT
        pkv_cast = [0]
        qkv_mm_tok = [0] * NPAIR
        qkv_ev_tok = [0] * NPAIR
        attn_rd_tok = [0] * NPAIR
        qkv_ev_by_g = {}
        retmm_tok = {}
        ret_ev_by_rg = {}
        inner_wait_v = {}
        inner_wait_s = {}
        inner_mm = {}
        cross_tok = [0] * NH
        head_ev_tok = [0] * NH
        vstat_tok = [0] * NT
        gnsc_tok = [0] * NT
        ln1_x2_tok = [0] * NT
        ln1_sq_tok = [0] * NT
        ln1_done_tok = [0] * NT
        stat2_tok = [0]
        stat_mm_tok = [0]
        gn_var_tok = [0]
        gn_std_tok = [0]
        gn_rstd_tok = [0]
        s_attn_done = [0]
        t_attn_done = [0]
        v_attn_done = [0]
        trx_tok = [0] * NT
        x1t_pre = [0] * NT
        sc_mm = [0] * NT
        mpr_tok = [0] * NT
        rmm_tok = [0] * NT
        posi_tok = [0]
        x1b_cast = [0] * NT
        hmm_tok = {}
        gelu_tok = {}
        w2g_tok = {}
        rnd_cons = [0] * NROUND
        qacc_tok = {}
        ln2_x2 = [0] * NT
        ln2_sq = [0] * NT
        ln2_done = [0] * NT

        const_pairs = [
            (ident_bf[:], ident_bf_c[:]), (ident_f[:], ident_f_c[:]),
            (trimask[:], trimask_c[:]), (striu[:], striu_c[:]),
            (onesall[:], onesall_c[:]),
            (ones128[:], ones128_c[:]), (zeros[:], zeros_c[:]),
            (gpow[:], gpow_c[:]), (gnw[:], gnw_h[:]), (gnb[:], gnb_h[:]),
            (wr[:], wr_h[:]), (brb[:], brb_h[:]), (b1r[:], b1r_h[:]),
            (epsc[:], epsc_c[:]), (ecol[:], ecol_c[:]), (iotar[:], iotar_c[:]),
            (zbf[:], zbf_c[:]),
            (scr[:, 0:NH * HD], pkv_h[:]),
        ]

        def make_plans():
            const_tok = None
            for d, s in const_pairs:
                const_tok, _ = sync_dma(sconst, d, s)
            xinit_tok, xinit_ent = [0] * NT, [None] * NT
            for tj in range(NT):
                xinit_tok[tj], xinit_ent[tj] = sync_dma(
                    sx[tj % 2],
                    x_st[:, (tj % 2) * D:(tj % 2 + 1) * D],
                    x_h[tj * 128:(tj + 1) * 128, :])
            gneg_tok, gneg_ent = [0] * NPAIR, [None] * NPAIR
            for hp in range(NPAIR):
                gneg_tok[hp], gneg_ent[hp] = sync_dma(sgn[hp % 2],
                                                      gneg[hp % 2][:], gneg_c[hp])
            for k in range(NSI):
                sync_dma(sxz, xs_d[k * 128:(k + 1) * 128, :], zbf[:])
            xrel_tok, xrel_ent = [0] * NT, [None] * NT
            for tj in range(NT):
                xrel_tok[tj], xrel_ent[tj] = sync_dma(
                    sx[tj % 2],
                    x_st[:, (tj % 2) * D:(tj % 2 + 1) * D],
                    x_h[tj * 128:(tj + 1) * 128, :])
            xsT_tok, xsT_ent = [0] * E, [None] * E
            for e_ in range(E if upto != "ln1" else 0):
                ents = []
                for ct in range(NCT):
                    xsT_tok[e_], ent = sync_dma(
                        sxt[e_ % 2],
                        xsT[e_ % 2][:, ct * 512:(ct + 1) * 512],
                        xs_d[e_ * CAP:(e_ + 1) * CAP, ct * 128:(ct + 1) * 128],
                        transpose=True)
                    ents.append(ent)
                xsT_ent[e_] = ents
            out_ent = [None] * NT
            for tj in range(NT):
                _, out_ent[tj] = sync_dma(sst, out_h[tj * 128:(tj + 1) * 128, :],
                                          racc[:, tj * D:(tj + 1) * D])
            wq_tok, wq_ent = [0] * NPAIR, [None] * NPAIR
            for hp in range(NPAIR):
                ents = []
                for g, cpt in enumerate((hp, 8 + hp, 16 + hp)):
                    src = wqkv_h[:, cpt * 128:(cpt + 1) * 128].rearrange(
                        "(ct p) u -> p ct u", p=128)
                    dst = wq[hp % 2].ap()[:, g * 1024:(g + 1) * 1024].rearrange(
                        "p (ct u) -> p ct u", u=128)
                    wq_tok[hp], e = gp_dma(swq[hp % 2], dst, src)
                    ents.append(e)
                wq_ent[hp] = ents
            scat_tok, scat_ent = [0] * NT, [None] * NT
            ya_tok, ya_ent = {}, {}
            gat_tok, gat_ent = [0] * NT, [None] * NT
            b2_tok, b2_ent = [0] * E, [None] * E
            rnd_tok, rnd_ent = [0] * NROUND, [None] * NROUND
            if upto != "ln1":
                for tj in range(NT):
                    scat_tok[tj], scat_ent[tj] = gp_dma(
                        ssc[tj % 2], xs_d[:], x1bft[tj % 2][:], kind="ind",
                        out_offset=IOA(posi[:, tj:tj + 1], 0))
                def emit_ya(e_, qq):
                    for si in range(4):
                        ya_tok[(e_, qq, si)], ya_ent[(e_, qq, si)] = gp_dma(
                            sya[si], ys_d[:],
                            ys_qacc[:, si * D:(si + 1) * D],
                            kind="ind",
                            out_offset=IOA(
                                iotar[:, e_ * 4 + si:e_ * 4 + si + 1], 0),
                            compute_op=(ALU.bypass if qq == 0 else ALU.add))
                pend = {}
                for e_ in range(E):
                    b2_tok[e_], b2_ent[e_] = gp_dma(
                        sb2, b2aug[:], b2aug_h[:, e_ * D:(e_ + 1) * D])
                    for q_ in range(NQ):
                        r = e_ * NQ + q_
                        b = r % 2
                        src1 = w1_h[e_, :, q_ * 512:(q_ + 1) * 512].rearrange(
                            "(ct p) f -> p ct f", p=128)
                        dst1 = w1q[b].ap().rearrange("p (ct f) -> p ct f", f=512)
                        _, e1 = gp_dma(swt[b], dst1, src1)
                        src2 = w2_h[e_, q_ * 512:(q_ + 1) * 512, :].rearrange(
                            "(ft p) c -> p ft c", p=128)
                        dst2 = w2q[b].ap().rearrange("p (ft c) -> p ft c", c=1024)
                        rnd_tok[r], e2 = gp_dma(swt[b], dst2, src2)
                        rnd_ent[r] = (e1, e2)
                        if q_ % 2 == 1:
                            pend[r + 1] = (e_, q_ // 2)
                        if r in pend:
                            emit_ya(*pend.pop(r))
                for rel in sorted(pend):
                    emit_ya(*pend[rel])
                for tj in range(NT):
                    gat_tok[tj], gat_ent[tj] = gp_dma(
                        sga, racc[:, tj * D:(tj + 1) * D],
                        ys_d[0:E * CAP, :],
                        kind="ind",
                        in_offset=IOA(posi[:, tj:tj + 1], 0),
                        compute_op=ALU.add)
            return dict(const_tok=const_tok, xinit_tok=xinit_tok, xinit_ent=xinit_ent,
                        gneg_tok=gneg_tok, gneg_ent=gneg_ent, xrel_tok=xrel_tok,
                        xrel_ent=xrel_ent, out_ent=out_ent, wq_tok=wq_tok,
                        wq_ent=wq_ent, rnd_tok=rnd_tok, rnd_ent=rnd_ent,
                        b2_tok=b2_tok, b2_ent=b2_ent, xsT_tok=xsT_tok,
                        xsT_ent=xsT_ent, scat_tok=scat_tok, scat_ent=scat_ent,
                        ya_tok=ya_tok, ya_ent=ya_ent, gat_tok=gat_tok,
                        gat_ent=gat_ent)

        def phA_scalar(scalar, P):
            w = wmk("s", scalar)
            w(*P["const_tok"])
            scalar.mul(pkv_bf[:], scr[:, 0:NH * HD], float(NH * HD))
            pkv_cast[0] = sk.n
            for tj in range(NT):
                w(*P["xinit_tok"][tj])
                if tj >= 1:
                    w(tk, trmm_tok[tj - 1])
                sl = (tj % 2) * D
                scalar.copy(xbf_st[:, 0:D], x_st[:, sl:sl + D])
                cast_tok[tj] = sk.n

        def phA_tensor(tensor, P):
            w = wmk("t", tensor)
            for tj in range(NT):
                w(sk, cast_tok[tj])
                if tj >= 2:
                    w(vk, trev_tok[tj - 2])
                pb = (tj % 2) * 1024
                for ct in range(NCT):
                    ins = tensor.transpose(
                        ps_trbf[:, pb + ct * 128:pb + (ct + 1) * 128],
                        xbf_st[:, ct * 128:(ct + 1) * 128], ident_bf[:])
                trmm_tok[tj] = tk.inc(ins)

        def phA_vector(vector, P):
            w = wmk("v", vector)
            for tj in range(NT):
                w(tk, trmm_tok[tj])
                pb = (tj % 2) * 1024
                src = ps_trbf[:, pb:pb + 1024].rearrange("p (ct u) -> p ct u", u=128)
                dst = xT.ap().rearrange("p (ct t) -> p ct t", t=T)[:, :, tj * 128:(tj + 1) * 128]
                vector.tensor_copy(dst, src)
                trev_tok[tj] = vk.n

        def _prev_key(h, i):
            if i >= 2:
                return (h, i - 2)
            if h >= 1:
                return (h - 1, NT - 2 + i)
            return None

        def phB_tensor(tensor, P):
            w = wmk("t", tensor)
            gctr = [0]
            rctr = [0]

            def qkv_pair(hp):
                bi = hp % 2
                w(*P["wq_tok"][hp])
                w(vk, trev_tok[NT - 1])
                if hp >= 2:
                    w(tk, attn_rd_tok[hp - 2])
                for g in range(2):
                    for c in range(4):
                        gidx = gctr[0]
                        w(vk, qkv_ev_by_g.get(gidx - 2, 0))
                        pbank = (gidx % 2) * 512
                        for ct in range(NCT):
                            ins = tensor.matmul(
                                ps_qkv[:, pbank:pbank + 512],
                                wq[bi][:, (g * 8 + ct) * 128:(g * 8 + ct + 1) * 128],
                                xT[:, ct * T + c * 512:ct * T + (c + 1) * 512],
                                start=(ct == 0), stop=(ct == NCT - 1))
                        retmm_tok[("qkv", hp, g, c)] = tk.inc(ins)
                        gctr[0] += 1
                for ti in range(NT):
                    gidx = gctr[0]
                    w(vk, qkv_ev_by_g.get(gidx - 2, 0))
                    pbank = (gidx % 2) * 512
                    for ct in range(NCT):
                        ins = tensor.matmul(
                            ps_qkv[:, pbank:pbank + 128],
                            xT[:, ct * T + ti * 128:ct * T + (ti + 1) * 128],
                            wq[bi][:, (2 * 8 + ct) * 128:(2 * 8 + ct + 1) * 128],
                            start=(ct == 0), stop=(ct == NCT - 1))
                    retmm_tok[("qkv", hp, 2, ti)] = tk.inc(ins)
                    gctr[0] += 1
                qkv_mm_tok[hp] = tk.n

            def attn_head(h):
                hp, po = h // 2, (h % 2) * 64
                bi = hp % 2
                psi = ps_inner if h % 2 == 0 else ps_inner2
                if h > 1:
                    w(vk, head_ev_tok[h - 2])
                tensor.matmul(psi[:, 0:512], zeros[0:1, 0:128],
                              zeros[0:1, 0:512], start=True, stop=True)
                tensor.matmul(psi[:, 512:1024], zeros[0:1, 0:128],
                              zeros[0:1, 0:512], start=True, stop=True)

                def ret_mms(i):
                    for c in range(i // 4, 4):
                        ridx = rctr[0]
                        wv, ws = ret_ev_by_rg.get(ridx - 2, (0, 0))
                        w(vk, wv)
                        w(sk, ws)
                        pbank = (ridx % 2) * 512
                        ins = tensor.matmul(
                            ps_ret[:, pbank:pbank + 512],
                            kT2[bi][po:po + 64, i * 128:(i + 1) * 128],
                            qT2[bi][po:po + 64, c * 512:(c + 1) * 512],
                            start=True, stop=True)
                        retmm_tok[(h, i, c)] = tk.inc(ins)
                        tok[("rbank", h, i, c)] = ridx
                        rctr[0] += 1

                def inner_mms(i):
                    w(vk, inner_wait_v.get((h, i), 0))
                    w(sk, inner_wait_s.get((h, i), 0))
                    for j in range(i, NT):
                        ins = tensor.matmul(
                            psi[:, j * 64:(j + 1) * 64],
                            ret_sb[i % 2][:, j * 128:(j + 1) * 128],
                            v2[bi][:, i * 128 + po:i * 128 + po + 64],
                            start=False, stop=False, skip_group_check=True)
                    inner_mm[(h, i)] = tk.inc(ins)

                ret_mms(0)
                for i in range(1, NT):
                    ret_mms(i)
                    inner_mms(i - 1)
                inner_mms(NT - 1)
                w(sk, pkv_cast[0])
                w(vk, qkv_ev_tok[hp])
                for j in range(NT):
                    ins = tensor.matmul(
                        psi[:, j * 64:(j + 1) * 64],
                        qhT2[bi][po:po + 64, j * 128:(j + 1) * 128],
                        pkv_bf[po:po + 64, h * 64:(h + 1) * 64],
                        start=False, stop=False, skip_group_check=True)
                cross_tok[h] = tk.inc(ins)

            qkv_pair(0)
            for hp in range(NPAIR):
                if hp + 1 < NPAIR:
                    qkv_pair(hp + 1)
                w(vk, qkv_ev_tok[hp])
                attn_head(2 * hp)
                attn_head(2 * hp + 1)
                attn_rd_tok[hp] = tk.n
            t_attn_done[0] = tk.n

        def phB_vector(vector, P):
            w = wmk("v", vector)
            gctr = [0]

            def qkv_evacs(hp):
                bi = hp % 2
                w(*P["gneg_tok"][hp])
                if hp >= 2:
                    w(tk, attn_rd_tok[hp - 2])
                for g in range(2):
                    for c in range(4):
                        w(tk, retmm_tok[("qkv", hp, g, c)])
                        pbank = (gctr[0] % 2) * 512
                        sl = slice(c * 512, (c + 1) * 512)
                        if g == 0:
                            vector.tensor_copy(qT2[bi][:, sl], ps_qkv[:, pbank:pbank + 512])
                            ins = vector.tensor_mul(qhT2[bi][:, sl],
                                                    ps_qkv[:, pbank:pbank + 512],
                                                    gneg[hp % 2][:, sl])
                        else:
                            ins = vector.tensor_mul(kT2[bi][:, sl],
                                                    ps_qkv[:, pbank:pbank + 512],
                                                    gneg[hp % 2][:, sl])
                        qkv_ev_by_g[gctr[0]] = vk.n
                        gctr[0] += 1
                for ti in range(NT):
                    w(tk, retmm_tok[("qkv", hp, 2, ti)])
                    pbank = (gctr[0] % 2) * 512
                    ins = vector.tensor_copy(v2[bi][:, ti * 128:(ti + 1) * 128],
                                             ps_qkv[:, pbank:pbank + 128])
                    qkv_ev_by_g[gctr[0]] = vk.n
                    gctr[0] += 1
                qkv_ev_tok[hp] = vk.n
                if PASS[0] == 1 and hp + 2 < NPAIR:
                    pwait(P["gneg_ent"][hp + 2], vk, vk.n)

            def attn_evacs(hp):
                for h in (2 * hp, 2 * hp + 1):
                    for i in range(NT):
                        c = i // 4
                        pk = _prev_key(h, i)
                        if pk is not None:
                            w(tk, inner_mm.get(pk, 0))
                        w(tk, retmm_tok[(h, i, c)])
                        ridx = tok[("rbank", h, i, c)]
                        pbank = (ridx % 2) * 512
                        off = (i - 4 * c) * 128
                        ins = vector.tensor_mul(
                            ret_sb[i % 2][:, i * 128:(i + 1) * 128],
                            ps_ret[:, pbank + off:pbank + off + 128], trimask[:])
                        if i + 1 < 4 * c + 4:
                            off2 = off + 128
                            ins = vector.tensor_copy(
                                ret_sb[i % 2][:, (i + 1) * 128:(4 * c + 4) * 128],
                                ps_ret[:, pbank + off2:pbank + 512])
                        t_ = vk.n
                        ret_ev_by_rg[ridx] = (t_, ret_ev_by_rg.get(ridx, (0, 0))[1])
                        inner_wait_v[(h, i)] = t_
                    w(tk, cross_tok[h])
                    psi = ps_inner if h % 2 == 0 else ps_inner2
                    for j in range(NT):
                        ins = vector.tensor_scalar_mul(
                            racc[:, j * D + h * 64:j * D + (h + 1) * 64],
                            psi[:, j * 64:(j + 1) * 64],
                            gpow[:, h * NT + j:h * NT + j + 1])
                    head_ev_tok[h] = vk.n

            qkv_evacs(0)
            for hp in range(NPAIR):
                if hp + 1 < NPAIR:
                    qkv_evacs(hp + 1)
                attn_evacs(hp)
            v_attn_done[0] = vk.n

        def phB_scalar(scalar, P):
            w = wmk("s", scalar)
            for hp in range(NPAIR):
                for h in (2 * hp, 2 * hp + 1):
                    for i in range(NT):
                        c0 = i // 4
                        for c in range(c0 + 1, 4):
                            pk = _prev_key(h, i)
                            if pk is not None:
                                w(tk, inner_mm.get(pk, 0))
                            w(tk, retmm_tok[(h, i, c)])
                            ridx = tok[("rbank", h, i, c)]
                            pbank = (ridx % 2) * 512
                            scalar.copy(
                                ret_sb[i % 2][:, 4 * c * 128:(4 * c + 4) * 128],
                                ps_ret[:, pbank:pbank + 512])
                            t_ = sk.n
                            ret_ev_by_rg[ridx] = (ret_ev_by_rg.get(ridx, (0, 0))[0], t_)
                            inner_wait_s[(h, i)] = t_
            s_attn_done[0] = sk.n

        def _x1t_evac(vector, w, tj):
            sl = (tj % 2) * D
            w(tk, t_attn_done[0])
            w(tk, trx_tok[tj])
            if tj >= 2:
                w(tk, sc_mm[tj - 2])
            pb = (tj % 2) * 1024
            vector.tensor_copy(scr[:, sl:sl + D], ps_y[:, pb:pb + 1024])
            x1t_pre[tj] = vk.n

        def phC_vector(vector, P):
            w = wmk("v", vector)
            w(vk, head_ev_tok[NH - 1])
            for tj in range(NT):
                ins = vector.tensor_reduce(sum_p[:, tj:tj + 1],
                                           racc[:, tj * D:(tj + 1) * D], AX, ALU.add)
            vector.tensor_reduce(stat2[:, 0:1], sum_p[:], AX, ALU.add)
            w(sk, tok.get("gn_sq", 0))
            ins = vector.tensor_reduce(stat2[:, 1:2], sq_p[:], AX, ALU.add)
            stat2_tok[0] = vk.n
            w(tk, stat_mm_tok[0])
            vector.tensor_scalar_mul(stat_m[:], ps_inner[:, 0:2], 1.0 / (T * D))
            vector.tensor_mul(gvar[:], stat_m[:, 0:1], stat_m[:, 0:1])
            vector.tensor_sub(gvar[:], stat_m[:, 1:2], gvar[:])
            gn_var_tok[0] = vk.n
            w(sk, gn_std_tok[0])
            vector.reciprocal(grstd[:], grstd[:])
            gn_rstd_tok[0] = vk.n
            for tj in range(NT):
                sl = (tj % 2) * D
                lb = (tj % 2) * 4
                vector.tensor_scalar(scr[:, sl:sl + D], racc[:, tj * D:(tj + 1) * D],
                                     stat_m[:, 0:1], grstd[:], ALU.subtract, ALU.mult)
                vstat_tok[tj] = vk.n
                for h in range(NH // 2):
                    vector.tensor_scalar(scr[:, sl + h * 64:sl + (h + 1) * 64],
                                         scr[:, sl + h * 64:sl + (h + 1) * 64],
                                         gnw[:, h:h + 1], gnb[:, h:h + 1],
                                         ALU.mult, ALU.add)
                w(sk, gnsc_tok[tj])
                w(*P["xrel_tok"][tj])
                vector.tensor_add(scr[:, sl:sl + D], scr[:, sl:sl + D],
                                  x_st[:, sl:sl + D])
                vector.tensor_reduce(lncol[:, lb:lb + 1], scr[:, sl:sl + D], AX, ALU.add)
                vector.tensor_scalar_mul(lncol[:, lb:lb + 1], lncol[:, lb:lb + 1], 1.0 / D)
                vector.tensor_scalar_sub(
                    scr[:, sl:sl + D], scr[:, sl:sl + D], lncol[:, lb:lb + 1])
                ln1_x2_tok[tj] = vk.n
                if PASS[0] == 1 and tj + 2 < NT:
                    pwait(P["xrel_ent"][tj + 2], vk, ln1_x2_tok[tj])
                w(sk, ln1_sq_tok[tj])
                vector.reciprocal(lncol[:, lb + 1:lb + 2], lncol[:, lb + 2:lb + 3])
                vector.tensor_scalar_mul(
                    racc[:, tj * D:(tj + 1) * D], scr[:, sl:sl + D],
                    lncol[:, lb + 1:lb + 2])
                ln1_done_tok[tj] = vk.n
                if upto != "ln1":
                    _x1t_evac(vector, w, tj)

        def phC_scalar(scalar, P):
            w = wmk("s", scalar)
            w(vk, head_ev_tok[NH - 1])
            tok["sqC"] = 0
            for tj in range(NT):
                w(sk, tok["sqC"])
                scalar.activation(xbf_st[:, 0:D], racc[:, tj * D:(tj + 1) * D],
                                  AF.Square, accum_out=sq_p[:, tj:tj + 1])
                tok["sqC"] = sk.n
            tok["gn_sq"] = sk.n
            w(vk, gn_var_tok[0])
            scalar.activation(grstd[:], gvar[:], AF.Sqrt, bias=epsc[:])
            gn_std_tok[0] = sk.n
            for tj in range(NT):
                sl = (tj % 2) * D
                lb = (tj % 2) * 4
                w(vk, vstat_tok[tj])
                for h in range(NH // 2, NH):
                    scalar.activation(
                        scr[:, sl + h * 64:sl + (h + 1) * 64],
                        scr[:, sl + h * 64:sl + (h + 1) * 64],
                        AF.Identity, bias=gnb[:, h:h + 1], scale=gnw[:, h:h + 1])
                gnsc_tok[tj] = sk.n
                w(vk, ln1_x2_tok[tj])
                w(sk, tok["sqC"])
                scalar.activation(
                    xbf_st[:, 0:D], scr[:, sl:sl + D], AF.Square,
                    accum_out=lncol[:, lb + 2:lb + 3])
                tok["sqC"] = sk.n
                scalar.activation(
                    lncol[:, lb + 2:lb + 3], lncol[:, lb + 2:lb + 3], AF.Sqrt,
                    bias=epsc[:], scale=1.0 / D)
                ln1_sq_tok[tj] = sk.n

        def phC_tensor(tensor, P):
            w = wmk("t", tensor)
            w(vk, stat2_tok[0])
            ins = tensor.matmul(ps_inner[:, 0:2], ones128[:], stat2[:],
                                start=True, stop=True)
            stat_mm_tok[0] = tk.inc(ins)

        def phD_tensor(tensor, P):
            w = wmk("t", tensor)
            w(sk, s_attn_done[0])
            w(vk, v_attn_done[0])
            tensor.matmul(ps_sc[:], zeros[0:1, 0:128], zeros[0:1, 0:128],
                          start=True, stop=True)
            for tj in range(NT):
                w(vk, ln1_done_tok[tj])
                if tj >= 2:
                    w(vk, x1t_pre[tj - 2])
                pb = (tj % 2) * 1024
                for ct in range(NCT):
                    ins = tensor.transpose(
                        ps_y[:, pb + ct * 128:pb + (ct + 1) * 128],
                        racc[:, tj * D + ct * 128:tj * D + (ct + 1) * 128],
                        ident_f[:])
                trx_tok[tj] = tk.inc(ins)
                w(vk, x1t_pre[tj])
                sl = (tj % 2) * D
                for ct in range(NCT):
                    ins = tensor.matmul(ps_sc[:, tj * 8:(tj + 1) * 8],
                                        scr[:, sl + ct * 128:sl + (ct + 1) * 128],
                                        wr[:, ct * 8:(ct + 1) * 8],
                                        start=False, stop=(ct == NCT - 1),
                                        skip_group_check=True)
                sc_mm[tj] = tk.inc(ins)
            for tj in range(NT):
                w(vk, mpr_tok[tj])
                if tj >= 2:
                    w(vk, tok.get(("rev", tj - 2), 0))
                for jp in range(tj + 1):
                    ins = tensor.matmul(
                        ps_r[tj % 2][:],
                        striu[:] if jp == tj else onesall[:],
                        mprb[:, jp * E:(jp + 1) * E],
                        start=(jp == 0), stop=(jp == tj))
                rmm_tok[tj] = tk.inc(ins)

        def phD_vector(vector, P):
            w = wmk("v", vector)
            for tj in range(NT):
                w(tk, sc_mm[tj])
                vector.tensor_add(scores[:, tj * E:(tj + 1) * E],
                                  ps_sc[:, tj * E:(tj + 1) * E], brb[:])
                vector.tensor_reduce(m16[:, tj:tj + 1],
                                     scores[:, tj * E:(tj + 1) * E], AX, ALU.max)
                vector.tensor_scalar_sub(dcol[:], scores[:, tj * E:(tj + 1) * E],
                                         m16[:, tj:tj + 1])
                vector.tensor_single_scalar(
                    mprf[:, tj * E:(tj + 1) * E], dcol[:], 0.0, ALU.is_ge)
                vector.tensor_copy(csum[:, 0:1], mprf[:, tj * E:tj * E + 1])
                for e_ in range(1, E):
                    vector.tensor_add(csum[:, e_:e_ + 1], csum[:, e_ - 1:e_],
                                      mprf[:, tj * E + e_:tj * E + e_ + 1])
                vector.tensor_single_scalar(dcol[:], csum[:], 1.0, ALU.is_equal)
                vector.tensor_mul(mprf[:, tj * E:(tj + 1) * E],
                                  mprf[:, tj * E:(tj + 1) * E], dcol[:])
                vector.tensor_copy(mprb[:, tj * E:(tj + 1) * E],
                                   mprf[:, tj * E:(tj + 1) * E])
                mpr_tok[tj] = vk.n
            for tj in range(NT):
                w(tk, rmm_tok[tj])
                vector.tensor_copy(Rbuf[:, tj * E:(tj + 1) * E], ps_r[tj % 2][:])
                tok[("rev", tj)] = vk.n
                vector.tensor_mul(dcol[:], Rbuf[:, tj * E:(tj + 1) * E],
                                  mprf[:, tj * E:(tj + 1) * E])
                vector.tensor_reduce(pcol[:, 0:1], dcol[:], AX, ALU.add)
                vector.tensor_mul(dcol[:], mprf[:, tj * E:(tj + 1) * E], ecol[:])
                vector.tensor_reduce(pcol[:, 1:2], dcol[:], AX, ALU.add)
                vector.tensor_add(posf[:, tj:tj + 1], pcol[:, 0:1], pcol[:, 1:2])
            vector.tensor_copy(posi[:], posf[:])
            posi_tok[0] = vk.n

        def phD_scalar(scalar, P):
            w = wmk("s", scalar)
            for tj in range(NT):
                w(vk, ln1_done_tok[tj])
                if tj >= 2:
                    w(ssc[tj % 2], 16 * ((tj - 2) // 2 + 1))
                scalar.copy(x1bft[tj % 2][:], racc[:, tj * D:(tj + 1) * D])
                x1b_cast[tj] = sk.n

        def phE_tensor(tensor, P):
            w = wmk("t", tensor)
            for r in range(NROUND):
                e_, q_ = r // NQ, r % NQ
                b = r % 2
                w(*P["rnd_tok"][r])
                w(*P["xsT_tok"][e_])
                if q_ == NQ - 1:
                    w(*P["b2_tok"][e_])
                for ft in range(4):
                    w(sk, gelu_tok.get((r, ft - 2), 0))
                    if ft < 2:
                        w(sk, gelu_tok.get((r - 1, ft + 2), 0))
                    ph = ps_h2[ft % 2]
                    for ct in range(NCT):
                        ins = tensor.matmul(
                            ph[:],
                            w1q[b][:, ct * 512 + ft * 128:ct * 512 + (ft + 1) * 128],
                            xsT[e_ % 2][:, ct * 512:(ct + 1) * 512],
                            start=(ct == 0), stop=(ct == NCT - 1))
                    hmm_tok[(r, ft)] = tk.inc(ins)
                for si in range(4):
                    for chg in range(2):
                        gidx = r * 8 + si * 2 + chg
                        w(vk, qacc_tok.get(gidx - 4, 0))
                        pg = ps_yg[gidx % 4]
                        for ft in range(4):
                            w(sk, gelu_tok.get((r, ft), 0))
                            last = (ft == 3) and (q_ != NQ - 1)
                            ins = tensor.matmul(
                                pg[:],
                                hbuf[b][:, ft * 512 + si * 128:ft * 512 + si * 128 + 128],
                                w2q[b][:, ft * 1024 + chg * 512:ft * 1024 + chg * 512 + 512],
                                start=(ft == 0), stop=last)
                        if q_ == NQ - 1:
                            ins = tensor.matmul(
                                pg[:], onesall[:],
                                b2aug[:, chg * 512:chg * 512 + 512],
                                start=False, stop=True)
                        w2g_tok[(r, si, chg)] = tk.inc(ins)
                rnd_cons[r] = tk.n
                if PASS[0] == 1:
                    if r + 2 < NROUND:
                        pwait(P["rnd_ent"][r + 2][0], tk, rnd_cons[r])
                    if q_ == NQ - 1:
                        if e_ + 2 < E:
                            for ent in P["xsT_ent"][e_ + 2]:
                                pwait(ent, tk, rnd_cons[r])
                        if e_ + 1 < E:
                            pwait(P["b2_ent"][e_ + 1], tk, rnd_cons[r])

        def phE_scalar(scalar, P):
            w = wmk("s", scalar)
            for r in range(NROUND):
                e_, q_ = r // NQ, r % NQ
                b = r % 2
                for ft in range(4):
                    w(tk, hmm_tok[(r, ft)])
                    w(tk, rnd_cons[r - 2] if r >= 2 else 0)
                    scalar.activation(
                        hbuf[b][:, ft * 512:(ft + 1) * 512], ps_h2[ft % 2][:],
                        AF.Gelu,
                        bias=b1r[:, e_ * 32 + q_ * 4 + ft:e_ * 32 + q_ * 4 + ft + 1])
                    gelu_tok[(r, ft)] = sk.n

        def phE_vector(vector, P):
            w = wmk("v", vector)
            for r in range(NROUND):
                e_, q_ = r // NQ, r % NQ
                qq, sub = q_ // 2, q_ % 2
                for si in range(4):
                    for chg in range(2):
                        gidx = r * 8 + si * 2 + chg
                        w(tk, w2g_tok[(r, si, chg)])
                        if sub == 0 and e_ * 4 + qq >= 1:
                            w(sya[si], 16 * (e_ * 4 + qq))
                        pg = ps_yg[gidx % 4]
                        dst = ys_qacc[:, si * D + chg * 512:si * D + chg * 512 + 512]
                        if sub == 0:
                            vector.tensor_copy(dst, pg[:])
                        else:
                            vector.tensor_add(dst, pg[:], dst)
                        qacc_tok[gidx] = vk.n
                        if PASS[0] == 1 and sub == 1 and chg == 1:
                            pwait(P["ya_ent"][(e_, qq, si)], vk, vk.n)

        def phF_vector(vector, P):
            w = wmk("v", vector)
            for tj in range(NT):
                sl = (tj % 2) * D
                lb = (tj % 2) * 4
                if upto != "ln1":
                    w(sga, 16 * NT)
                vector.tensor_reduce(lncol[:, lb:lb + 1],
                                     racc[:, tj * D:(tj + 1) * D], AX, ALU.add)
                vector.tensor_scalar_mul(lncol[:, lb:lb + 1], lncol[:, lb:lb + 1],
                                         1.0 / D)
                vector.tensor_scalar_sub(
                    scr[:, sl:sl + D], racc[:, tj * D:(tj + 1) * D],
                    lncol[:, lb:lb + 1])
                ln2_x2[tj] = vk.n
                w(sk, ln2_sq[tj])
                vector.reciprocal(lncol[:, lb + 1:lb + 2], lncol[:, lb + 2:lb + 3])
                vector.tensor_scalar_mul(
                    racc[:, tj * D:(tj + 1) * D], scr[:, sl:sl + D],
                    lncol[:, lb + 1:lb + 2])
                ln2_done[tj] = vk.n
                if PASS[0] == 1:
                    pwait(P["out_ent"][tj], vk, ln2_done[tj])

        def phF_scalar(scalar, P):
            w = wmk("s", scalar)
            tok["sqF"] = 0
            for tj in range(NT):
                sl = (tj % 2) * D
                lb = (tj % 2) * 4
                w(vk, ln2_x2[tj])
                w(sk, tok.get("sqF", 0))
                scalar.activation(
                    xbf_st[:, 0:D], scr[:, sl:sl + D], AF.Square,
                    accum_out=lncol[:, lb + 2:lb + 3])
                tok["sqF"] = sk.n
                scalar.activation(
                    lncol[:, lb + 2:lb + 3], lncol[:, lb + 2:lb + 3], AF.Sqrt,
                    bias=epsc[:], scale=1.0 / D)
                ln2_sq[tj] = sk.n

        def ln1_debug_stores(P):
            if PASS[0] == 1:
                for tj in range(NT):
                    pwait(P["out_ent"][tj], vk, ln1_done_tok[tj])

        P = None
        for pno in (1, 2):
            PASS[0] = pno
            for c in all_dma_ctrs + [tk, vk, sk]:
                c.n = 0
            pidx[0] = pidx[1] = 0
            last_wait.clear()
            Pnew = make_plans()
            if P is None:
                P = Pnew
            if pno == 1:
                m = _ME()
                mv = _Wrap(_ME(), vk, PASS)
                ms = _Wrap(_ME(), sk, PASS)
                phA_scalar(ms, P)
                phA_tensor(m, P)
                phA_vector(mv, P)
                for tj in range(2, NT):
                    pwait(P["xinit_ent"][tj], sk, cast_tok[tj - 2])
                phB_tensor(m, P)
                phB_vector(mv, P)
                phB_scalar(ms, P)
                for hp in range(2, NPAIR):
                    for e in P["wq_ent"][hp]:
                        pwait(e, tk, qkv_mm_tok[hp - 2])
                if upto != "ln1":
                    for r in (0, 1):
                        pwait(P["rnd_ent"][r][0], tk, t_attn_done[0])
                        pwait(P["rnd_ent"][r][0], vk, v_attn_done[0])
                        pwait(P["rnd_ent"][r][0], sk, s_attn_done[0])
                    for e_ in (0, 1):
                        for ent in P["xsT_ent"][e_]:
                            pwait(ent, tk, t_attn_done[0])
                            pwait(ent, vk, v_attn_done[0])
                            pwait(ent, sk, s_attn_done[0])
                    pwait(P["b2_ent"][0], tk, t_attn_done[0])
                    pwait(P["b2_ent"][0], vk, v_attn_done[0])
                    pwait(P["b2_ent"][0], sk, s_attn_done[0])
                for tj in range(NT):
                    pwait(P["xrel_ent"][tj], sk, cast_tok[14 + (tj % 2)])
                phC_vector(mv, P)
                phC_scalar(ms, P)
                phC_tensor(m, P)
                if upto == "ln1":
                    ln1_debug_stores(P)
                else:
                    phD_tensor(m, P)
                    phD_vector(mv, P)
                    phD_scalar(ms, P)
                    for tj in range(NT):
                        ent = P["scat_ent"][tj]
                        pwait(ent, sxz, 16 * NSI)
                        pwait(ent, vk, posi_tok[0])
                        pwait(ent, sk, x1b_cast[tj])
                    for e_ in range(E):
                        for ent in P["xsT_ent"][e_]:
                            pwait(ent, ssc[0], 16 * (NT // 2))
                            pwait(ent, ssc[1], 16 * (NT // 2))
                    phE_tensor(m, P)
                    phE_scalar(ms, P)
                    phE_vector(mv, P)
                    for e_ in range(E):
                        for qq in range(4):
                            for si in range(4):
                                gq = e_ * 4 + qq
                                if gq >= 1:
                                    pwait(P["ya_ent"][(e_, qq, si)], sya[si],
                                          16 * gq)
                    for tj in range(NT):
                        for si in range(4):
                            pwait(P["gat_ent"][tj], sya[si], 16 * E * 4)
                    phF_vector(mv, P)
                    phF_scalar(ms, P)
            else:
                block = S.enter_context(nc.Block())

                def t_body(tensor):
                    phA_tensor(tensor, P)
                    phB_tensor(tensor, P)
                    phC_tensor(tensor, P)
                    if upto != "ln1":
                        phD_tensor(tensor, P)
                        phE_tensor(tensor, P)

                def v_body(vector):
                    vector = _Wrap(vector, vk, PASS)
                    phA_vector(vector, P)
                    phB_vector(vector, P)
                    phC_vector(vector, P)
                    if upto != "ln1":
                        phD_vector(vector, P)
                        phE_vector(vector, P)
                        phF_vector(vector, P)
                    elif upto == "ln1":
                        pass

                def s_body(scalar):
                    scalar = _Wrap(scalar, sk, PASS)
                    phA_scalar(scalar, P)
                    phB_scalar(scalar, P)
                    phC_scalar(scalar, P)
                    if upto != "ln1":
                        phD_scalar(scalar, P)
                        phE_scalar(scalar, P)
                        phF_scalar(scalar, P)

                def sync_body(sync):
                    for dst, src, waits, ctr, kind, kw in sync_plan:
                        for (c, v) in waits:
                            sync.wait_ge(c.h, v)
                        sync.dma_start(dst, src, **kw).then_inc(ctr.h, 16)
                    sync.wait_ge(sst.h, sst.n)

                def gp_body(gpsimd):
                    for dst, src, waits, ctr, kind, kw in gp_plan:
                        for (c, v) in waits:
                            gpsimd.wait_ge(c.h, v)
                        if kind == "dma":
                            gpsimd.dma_start(dst, src, **kw).then_inc(ctr.h, 16)
                        else:
                            gpsimd.indirect_dma_start(
                                out=dst, out_offset=kw.get("out_offset"),
                                in_=src, in_offset=kw.get("in_offset"),
                                compute_op=kw.get("compute_op", ALU.bypass),
                            ).then_inc(ctr.h, 16)
                    if upto != "ln1":
                        gpsimd.wait_ge(sga.h, sga.n)

                block.tensor(t_body)
                block.vector(v_body)
                block.scalar(s_body)
                block.sync(sync_body)
                block.gpsimd(gp_body)

    return nc


def _host_layout(inputs):
    x = np.asarray(inputs["L0_attn_x"], dtype=np.float32)
    pkv = np.asarray(inputs["L0_attn_past_kv"], dtype=np.float32)
    wqkv = np.ascontiguousarray(np.asarray(inputs["L0_attn_W_qkv"], dtype=np.float32))
    gnw = np.asarray(inputs["L0_attn_gn_weight"], dtype=np.float32)
    gnb = np.asarray(inputs["L0_attn_gn_bias"], dtype=np.float32)
    wrv = np.asarray(inputs["L0_ffn_wr"], dtype=np.float32)
    br = np.asarray(inputs["L0_ffn_br"], dtype=np.float32)
    w1 = np.ascontiguousarray(np.asarray(inputs["L0_ffn_w1"], dtype=np.float32))
    b1 = np.asarray(inputs["L0_ffn_b1"], dtype=np.float32)
    w2 = np.ascontiguousarray(np.asarray(inputs["L0_ffn_w2"], dtype=np.float32))
    b2 = np.asarray(inputs["L0_ffn_b2"], dtype=np.float32)

    pkv_r1 = pkv.transpose(1, 0, 2).reshape(HD, NH * HD)
    pkv_r = np.ascontiguousarray(np.vstack([pkv_r1, pkv_r1]))
    gnw_b = np.ascontiguousarray(np.broadcast_to(gnw[None, :], (128, NH)))
    gnb_b = np.ascontiguousarray(np.broadcast_to(gnb[None, :], (128, NH)))
    wr_r = np.ascontiguousarray(
        wrv.reshape(NCT, 128, E).transpose(1, 0, 2).reshape(128, NCT * E))
    brb = np.ascontiguousarray(np.broadcast_to(br[None, :], (128, E)))
    b1r = np.ascontiguousarray(
        b1.reshape(E, 32, 128).transpose(2, 0, 1).reshape(128, E * 32))
    b2aug = np.zeros((128, E * D), np.float32)
    b2aug[0, :] = b2.reshape(-1)

    shared = dict(wqkv=wqkv, pkv=pkv_r, gnw=gnw_b, gnb=gnb_b, wr=wr_r, brb=brb,
                  b1r=b1r, b2aug=b2aug, w1=w1, w2=w2)
    maps = []
    for i in range(NCORE):
        m = dict(shared)
        m["x"] = np.ascontiguousarray(x[i])
        maps.append(m)
    return maps


def kernel(**inputs):
    maps = _host_layout(inputs)
    nc = build("full")
    res = run_bass_kernel_spmd(nc, maps, core_ids=list(range(NCORE)))
    return np.stack([np.asarray(res.results[i]["out"], dtype=np.float32)
                     for i in range(NCORE)], axis=0)
